# version 06
# baseline (speedup 1.0000x reference)
import numpy as np
import concourse.bass as bass
import concourse.mybir as mybir
from concourse.bass_utils import run_bass_kernel_spmd
from contextlib import ExitStack

F32 = mybir.dt.float32
BF16 = mybir.dt.bfloat16
AF = mybir.ActivationFunctionType
ALU = mybir.AluOpType
AX = mybir.AxisListType

C = 2048
HID = 2560
KD = 16
KH = 20
KF = 11
FM = 1408
NE = 12
TC = 2048
TA = 512
NTA = 4
TF = 1024
NTF = 2
GC = 0.7978845608028654
ENGINES = ("sync", "gpsimd", "tensor", "vector", "scalar")
RG8 = [[0, 1, 2, 3, 4, 5, 6, 7]]


class B:

    def __init__(self):
        self.streams = {e: [] for e in ENGINES}
        self.cnt = {s: 0 for s in ("w", "x", "st", "mm", "sc", "v", "cc")}
        self.handles = {}
        self.last_wait = {e: {} for e in ENGINES}

    def emit(self, eng, fn):
        self.streams[eng].append(fn)

    def wait(self, eng, s, val):
        if val is None or val <= 0:
            return
        lw = self.last_wait[eng]
        if lw.get(s, -1) >= val:
            return
        lw[s] = val
        h = self.handles
        self.emit(eng, lambda e, s=s, v=val: e.wait_ge(h[s], v))

    def vbar(self):
        self.wait("vector", "v", self.cnt["v"])

    def barrier(self, eng, sems=("mm", "sc", "v")):
        for s in sems:
            self.wait(eng, s, self.cnt[s])

    def dma(self, eng, sname, out, in_):
        self.cnt[sname] += 16
        c = self.cnt[sname]
        h = self.handles
        self.emit(eng, lambda e, o=out, i=in_, s=sname: e.dma_start(out=o, in_=i)
                  .then_inc(h[s], 16))
        return c

    def op(self, eng, sname, fn, dep=None):
        if eng == "vector":
            assert sname == "v"
            return self.opn(eng, fn, dep=dep)
        self.cnt[sname] += 1
        c = self.cnt[sname]
        h = self.handles
        self.emit(eng, lambda e, fn=fn, s=sname: fn(e).then_inc(h[s], 1))
        return c

    def opn(self, eng, fn, dep=None):
        if eng == "vector":
            if dep is None:
                dep = self.cnt["v"]
            elif isinstance(dep, (list, tuple)):
                dep = max([d for d in dep if d] or [0])
            self.wait(eng, "v", dep)
            self.cnt["v"] += 1
            c = self.cnt["v"]
            h = self.handles
            self.emit(eng, lambda e, fn=fn: fn(e).then_inc(h["v"], 1))
            return c
        self.emit(eng, lambda e, fn=fn: fn(e))

    def run(self, block):
        for eng_name in ENGINES:
            ops = self.streams[eng_name]
            getattr(block, eng_name)(lambda e, ops=ops: [f(e) for f in ops])


def build_program(debug=False):
    nc = bass.Bass("TRN2", target_bir_lowering=False)
    dbg_kind = "ExternalOutput" if debug else "Internal"

    def din(name, shape):
        return nc.dram_tensor(name, list(shape), F32, kind="ExternalInput").ap()

    xin = din("xin", (C, TC))
    xpreT = din("xpreT", (KD, 128, 3))
    maskT = din("maskT", (128, 1))
    selT = din("selT", (8, 1))
    identT = din("identT", (128, 128))
    wr_t = din("wr_t", (KH, 128, C))
    wl_t = din("wl_t", (KH, 128, C))
    wa_t = din("wa_t", (KH, 128, HID))
    wi_t = din("wi_t", (KH, 128, HID))
    wo_t = din("wo_t", (KD, 128, HID))
    fc1_t = din("fc1_t", (NE, KF, 128, C))
    fc2_t = din("fc2_t", (NE, KF, 128, C))
    prj_t = din("prj_t", (NE, KD, 128, FM))
    gs_t = din("gs_t", (128, KD * 9))
    blT = din("blT", (128, KH))
    brT = din("brT", (128, KH))
    cbT = din("cbT", (128, KH))
    baT = din("baT", (128, KH))
    biT = din("biT", (128, KH))
    lamT = din("lamT", (128, KH))
    boT = din("boT", (128, KD))
    cwT = din("cwT", (128, 4 * KH))
    fconst = din("fconst", (128, 8))

    yout = nc.dram_tensor("yout", [C, TC], F32, kind="ExternalOutput").ap()

    xmid = nc.dram_tensor("xmid", [C, TC], F32, kind=dbg_kind).ap()
    rl_sc = nc.dram_tensor("rl_sc", [HID, TC], BF16, kind=dbg_kind).ap()
    ar_sc = nc.dram_tensor("ar_sc", [HID, TC], BF16, kind=dbg_kind).ap()
    rows_d = nc.dram_tensor("rows_d", [NTA, 1024], F32, kind=dbg_kind).ap()
    cc_in = nc.dram_tensor("cc_in", [1, HID], F32).ap()
    cc_out = nc.dram_tensor("cc_out", [8, HID], F32).ap()
    hsel_d = nc.dram_tensor("hsel_d", [1, HID], F32, kind=dbg_kind).ap()
    comb_d = nc.dram_tensor("comb_d", [72, 128], F32, kind=dbg_kind).ap()
    if debug:
        dbg_stash = nc.dram_tensor("dbg_stash", [128, 3072], F32, kind="ExternalOutput").ap()
        dbg_c = nc.dram_tensor("dbg_c", [128, 2 * KH], F32, kind="ExternalOutput").ap()
        dbg_pool = nc.dram_tensor("dbg_pool", [128, 6144], F32, kind="ExternalOutput").ap()

    es = ExitStack()

    def sbuf(name, shape, dtype=F32):
        return es.enter_context(nc.sbuf_tensor(name, list(shape), dtype))

    def psum(name, shape):
        return es.enter_context(nc.psum_tensor(name, list(shape), F32))

    with es:
        xt = sbuf("xt", (128, KD * 1024))
        xnt = sbuf("xnt", (128, KD * 1024), BF16)
        bigA = sbuf("bigA", (128, KF * 1024), BF16)
        bigB = sbuf("bigB", (128, KF * 1024), BF16)
        stash = sbuf("stash", (128, 4 * 512))
        stashz = sbuf("stashz", (128, 2 * 512), BF16)
        wbuf = sbuf("wbuf", (128, 4 * HID), BF16)
        pool = sbuf("pool", (128, 6144))
        rows = sbuf("rows", (1, 2048))
        gs_sb = sbuf("gs_sb", (128, KD * 9))
        ident = sbuf("ident", (128, 128))
        ones_k1 = sbuf("ones_k1", (1, 128))
        ones_col = sbuf("ones_col", (128, 1))
        c_bl = sbuf("c_bl", (128, KH))
        c_br = sbuf("c_br", (128, KH))
        c_cb = sbuf("c_cb", (128, KH))
        c_hba = sbuf("c_hba", (128, KH))
        c_hbi = sbuf("c_hbi", (128, KH))
        c_lam = sbuf("c_lam", (128, KH))
        c_c1 = sbuf("c_c1", (128, KH))
        c_c2 = sbuf("c_c2", (128, KH))
        c_bo = sbuf("c_bo", (128, KD))
        c_cw = sbuf("c_cw", (128, 4 * KH))
        c_mask = sbuf("c_mask", (128, 1))
        sel_sb = sbuf("sel_sb", (8, 1))
        carry = sbuf("carry", (128, KH))
        prefp = sbuf("prefp", (128, KH))
        rtail = sbuf("rtail", (128, 3 * KH))
        hsel = sbuf("hsel", (128, KH))
        fc_sb = sbuf("fc_sb", (128, 8))
        FVALS = (-1.0, 1.0, 1e-5, 0.5, GC, 0.0, 2.0, -2.0)
        for i, fv in enumerate(FVALS):
            nc.const_aps.register(F32, fv, fc_sb[:, i:i + 1])

        psA = psum("psA", (128, 1024))
        psB = psum("psB", (128, 1024))
        psC = psum("psC", (128, 1024))
        psD = psum("psD", (128, 1024))
        PST = {"A": psA, "B": psB, "C": psC, "D": psD}

        W = B()

        ps_reader = {}

        def regions(tn, lo, hi):
            out = []
            if lo < 512:
                out.append((tn, 0))
            if hi > 512:
                out.append((tn, 1))
            return out

        def pe_guard(tn, lo, hi):
            for r in regions(tn, lo, hi):
                if r in ps_reader:
                    s, cv = ps_reader[r]
                    W.wait("tensor", s, cv)

        def mark_read(tn, lo, hi, sem, cntv):
            for r in regions(tn, lo, hi):
                ps_reader[r] = (sem, cntv)

        def mm(tn, lo, hi, lhsT, rhs, start, stop, inc=False, p0=0, p1=128):
            t = PST[tn]
            if inc:
                W.cnt["mm"] += 1
                c = W.cnt["mm"]
                h = W.handles
                W.emit("tensor", lambda e, t=t, lo=lo, hi=hi, l=lhsT, r=rhs,
                       s1=start, s2=stop, p0=p0, p1=p1: e.matmul(
                           t[p0:p1, lo:hi], l, r, start=s1,
                           stop=s2).then_inc(h["mm"], 1))
                return c
            W.emit("tensor", lambda e, t=t, lo=lo, hi=hi, l=lhsT, r=rhs,
                   s1=start, s2=stop, p0=p0, p1=p1: e.matmul(
                       t[p0:p1, lo:hi], l, r, start=s1, stop=s2))
            return None

        strip_state = {"idx": 0, "free": []}

        def load_strip(src_ap, cols):
            idx = strip_state["idx"]
            strip_state["idx"] += 1
            buf = idx % 4
            if idx >= 4:
                W.wait("gpsimd", "mm", strip_state["free"][idx - 4])
            dst = wbuf[:, buf * HID:buf * HID + cols]
            W.cnt["w"] += 16
            wc = W.cnt["w"]
            h = W.handles
            W.emit("gpsimd", lambda e, d=dst, s=src_ap: e.dma_start(out=d, in_=s)
                   .then_inc(h["w"], 16))
            return buf, wc

        def strips_done(n):
            for _ in range(n):
                strip_state["free"].append(W.cnt["mm"])

        def wv(buf, k, n=128):
            return wbuf[:, buf * HID + k * 128:buf * HID + k * 128 + n]

        def pv(lo, n, p0=0, p1=128):
            return pool[p0:p1, lo:lo + n]

        for src, dst in ((blT, c_bl), (brT, c_br), (cbT, c_cb), (baT, c_hba),
                         (biT, c_hbi), (lamT, c_lam), (boT, c_bo), (cwT, c_cw),
                         (maskT, c_mask), (selT, sel_sb), (identT, ident),
                         (gs_t, gs_sb), (fconst, fc_sb)):
            W.dma("sync", "x", dst[:, :], src)
        for k in range(KD):
            W.dma("sync", "x", xt[:, k * 1024 + 512:k * 1024 + 515], xpreT[k])
        x_consts = W.cnt["x"]

        W.opn("vector", lambda e: e.memset(ones_k1[:, :], 1.0))
        W.opn("vector", lambda e: e.memset(ones_col[:, :], 1.0))
        W.opn("vector", lambda e: e.memset(carry[:, :], 0.0))
        W.opn("vector", lambda e: e.memset(prefp[:, :], 1.0))
        W.wait("vector", "x", x_consts)
        W.op("vector", "v", lambda e: e.tensor_scalar(
            c_hba[:, :], c_hba[:, :], 0.5, None, ALU.mult))
        W.op("vector", "v", lambda e: e.tensor_scalar(
            c_hbi[:, :], c_hbi[:, :], 0.5, None, ALU.mult))
        W.wait("scalar", "x", x_consts)
        W.op("scalar", "sc", lambda e: e.activation(
            c_c1[:, :], c_lam[:, :], AF.Exp, scale=-1.0))
        sc_sp = W.op("scalar", "sc", lambda e: e.activation(
            c_c1[:, :], c_c1[:, :], AF.Ln, bias=1.0))
        W.wait("vector", "sc", sc_sp)
        W.op("vector", "v", lambda e: e.tensor_scalar(
            c_c1[:, :], c_c1[:, :], -4.0, None, ALU.mult))
        W.op("vector", "v", lambda e: e.tensor_scalar(
            c_c2[:, :], c_c1[:, :], 2.0, None, ALU.mult))

        def ln_block(t0col, ncols, save_tile=None):
            W.barrier("scalar", ("mm", "v"))
            W.barrier("tensor", ("sc", "v"))
            W.barrier("vector", ("mm", "sc"))
            pe_guard("D", 0, 1024)
            last_sq = {}
            for k in range(KD):
                xck = xt[:, k * 1024 + t0col:k * 1024 + t0col + ncols]
                xq = pv(1542 + (k % 2) * 512, ncols)
                if k >= 2:
                    W.wait("scalar", "mm", last_sq[k - 2])
                csq = W.op("scalar", "sc", lambda e, o=xq, i=xck: e.activation(
                    o, i, AF.Square))
                W.wait("tensor", "sc", csq)
                mm("D", 0, ncols, ones_col[:, 0:1], xck, k == 0, k == KD - 1,
                   p0=0, p1=1)
                last_sq[k] = mm("D", 512, 512 + ncols, ones_col[:, 0:1], xq,
                                k == 0, k == KD - 1, inc=True, p0=0, p1=1)
            stats_done = W.cnt["mm"]
            mean_r = rows[0:1, 0:ncols]
            rstd_r = rows[0:1, 512:512 + ncols]
            m2_r = rows[0:1, 1024:1024 + ncols]
            var_r = rows[0:1, 1536:1536 + ncols]
            W.wait("vector", "mm", stats_done)
            v_mean = W.op("vector", "v", lambda e: e.tensor_scalar(
                mean_r, psD[0:1, 0:ncols], 1.0 / C, None, ALU.mult))
            W.wait("scalar", "v", v_mean)
            c_m2 = W.op("scalar", "sc", lambda e: e.activation(m2_r, mean_r, AF.Square))
            W.wait("vector", "sc", c_m2)
            v_var = W.op("vector", "v", lambda e: e.scalar_tensor_tensor(
                var_r, psD[0:1, 512:512 + ncols], 1.0 / C, m2_r,
                ALU.mult, ALU.subtract))
            W.wait("scalar", "v", v_var)
            c_sd = W.op("scalar", "sc", lambda e: e.activation(
                var_r, var_r, AF.Sqrt, bias=1e-5))
            W.wait("vector", "sc", c_sd)
            v_rst = W.op("vector", "v", lambda e: e.reciprocal(rstd_r, var_r))
            mark_read("D", 0, 1024, "v", v_var)
            if save_tile is not None:
                W.wait("scalar", "v", v_rst)
                W.dma("scalar", "st", rows_d[save_tile], rows[0:1, 0:1024])
            pe_guard("D", 0, 1024)
            W.wait("tensor", "v", v_rst)
            mm("D", 0, ncols, ones_k1[0:1, :], mean_r, True, True)
            bc = mm("D", 512, 512 + ncols, ones_k1[0:1, :], rstd_r, True, True,
                    inc=True)
            W.wait("vector", "mm", bc)
            v_last = 0
            lnp = {}
            for k in range(KD):
                xck = xt[:, k * 1024 + t0col:k * 1024 + t0col + ncols]
                t0 = pv(1030 + (k % 2) * 512, ncols)
                W.opn("vector", lambda e, t0=t0, x=xck: e.tensor_tensor(
                    t0, x, psD[:, 0:ncols], ALU.subtract),
                    dep=(None if k < 2 else [lnp[k - 2]]))
                v_last = W.opn("vector", lambda e, t0=t0, k=k: e.tensor_tensor(
                    xnt[:, k * 1024 + t0col:k * 1024 + t0col + ncols], t0,
                    psD[:, 512:512 + ncols], ALU.mult), dep=W.cnt["v"])
                lnp[k] = v_last
            mark_read("D", 0, 1024, "v", v_last)
            return v_last

        W.wait("scalar", "x", x_consts)
        W.wait("vector", "x", x_consts)
        W.wait("tensor", "x", x_consts)
        v_xnpre = ln_block(512, 3)

        spill_st = {}
        conv_v = {}
        mm_stats = [0] * NTA
        v_ln1 = [0] * NTA
        mm_wawi = [0] * NTA
        grp_v = {}

        def a1_load_ln(t):
            W.wait("sync", "v", v_ln1[t - 1] if t > 0 else 0)
            W.wait("sync", "mm", mm_stats[t - 1] if t > 0 else 0)
            xc = 0
            for k in range(KD):
                xc = W.dma("sync", "x", xt[:, k * 1024:k * 1024 + TA],
                           xin[128 * k:128 * (k + 1), t * TA:(t + 1) * TA])
            W.wait("scalar", "x", xc)
            W.wait("tensor", "x", xc)
            W.wait("vector", "x", xc)
            v_ln1[t] = ln_block(0, TA, save_tile=t)
            mm_stats[t] = W.cnt["mm"]

        a1_load_ln(0)
        for t in range(NTA):
            for o in range(KH):
                buf, wc = load_strip(wr_t[o], C)
                W.wait("tensor", "w", wc)
                W.wait("tensor", "v", v_ln1[t])
                lo = (o % 2) * 512
                pe_guard("A", lo, lo + TA)
                for k in range(KD):
                    g_mm = mm("A", lo, lo + TA, wv(buf, k),
                              xnt[:, k * 1024:k * 1024 + TA],
                              k == 0, k == KD - 1, inc=(k == KD - 1))
                p_mm = None
                if t == 0:
                    W.wait("tensor", "v", v_xnpre)
                    pe_guard("B", 512, 1024)
                    for k in range(KD):
                        p_mm = mm("B", 512, 515, wv(buf, k),
                                  xnt[:, k * 1024 + 512:k * 1024 + 515],
                                  k == 0, k == KD - 1, inc=(k == KD - 1))
                strips_done(1)
                rh = (o % 2) * 515
                W.wait("scalar", "mm", g_mm)
                if o >= 2:
                    W.wait("scalar", "v", conv_v[(t, o - 2)])
                e_sc = W.op("scalar", "sc", lambda e, rh=rh, lo=lo, o=o: e.activation(
                    pool[:, rh + 3:rh + 3 + TA], psA[:, lo:lo + TA], AF.Identity,
                    bias=c_br[:, o:o + 1]))
                mark_read("A", lo, lo + 512, "sc", e_sc)
                if t == 0:
                    W.wait("scalar", "mm", p_mm)
                    e_pre = W.op("scalar", "sc", lambda e, o=o: e.activation(
                        pool[:, 2566:2569], psB[:, 512:515], AF.Identity,
                        bias=c_br[:, o:o + 1]))
                    mark_read("B", 512, 1024, "sc", e_pre)
                    W.wait("vector", "sc", e_pre)
                    W.op("vector", "v", lambda e, o=o: e.tensor_scalar(
                        rtail[:, 3 * o:3 * o + 3], pool[:, 2566:2569],
                        c_mask[:, 0:1], None, ALU.mult))
                W.wait("vector", "sc", e_sc)
                W.opn("vector", lambda e, rh=rh, o=o: e.tensor_copy(
                    pool[:, rh:rh + 3], rtail[:, 3 * o:3 * o + 3]))
                W.opn("vector", lambda e, rh=rh, o=o: e.tensor_scalar(
                    pool[:, 1030:1030 + TA], pool[:, rh:rh + TA],
                    c_cw[:, o:o + 1], c_cb[:, o:o + 1], ALU.mult, ALU.add))
                for tp in range(1, 4):
                    W.opn("vector", lambda e, rh=rh, o=o, tp=tp: e.scalar_tensor_tensor(
                        pool[:, 1030:1030 + TA], pool[:, rh + tp:rh + tp + TA],
                        c_cw[:, tp * KH + o:tp * KH + o + 1],
                        pool[:, 1030:1030 + TA], ALU.mult, ALU.add))
                W.opn("vector", lambda e, rh=rh, o=o: e.tensor_copy(
                    rtail[:, 3 * o:3 * o + 3], pool[:, rh + TA:rh + TA + 3]))
                if t > 0:
                    W.wait("vector", "mm", mm_wawi[t - 1])
                conv_v[(t, o)] = W.op("vector", "v", lambda e, o=o: e.tensor_copy(
                    bigA[:, o * 512:(o + 1) * 512], pool[:, 1030:1030 + TA]))
            v_conv_all = W.cnt["v"]
            if t + 1 < NTA:
                W.vbar()
                a1_load_ln(t + 1)

            for grp in range(KH // 2):
                if grp == 0:
                    W.wait("scalar", "v", v_conv_all)
                else:
                    W.wait("scalar", "v", grp_v[(t, grp - 1)])
                mm_r = {}
                mm_i = {}
                for j in range(2):
                    o = grp * 2 + j
                    bufA, wA = load_strip(wa_t[o], HID)
                    bufI, wI = load_strip(wi_t[o], HID)
                    lo = (o % 2) * 512
                    W.wait("tensor", "w", wI)
                    W.wait("tensor", "v", v_conv_all)
                    pe_guard("B", lo, lo + TA)
                    for k in range(KH):
                        mm_r[j] = mm("B", lo, lo + TA, wv(bufA, k),
                                     bigA[:, k * 512:k * 512 + TA],
                                     k == 0, k == KH - 1, inc=(k == KH - 1))
                    pe_guard("C", lo, lo + TA)
                    for k in range(KH):
                        mm_i[j] = mm("C", lo, lo + TA, wv(bufI, k),
                                     bigA[:, k * 512:k * 512 + TA],
                                     k == 0, k == KH - 1, inc=(k == KH - 1))
                    strips_done(2)
                for j in range(2):
                    o = grp * 2 + j
                    lo = (o % 2) * 512
                    zsl = stashz[:, j * 512:(j + 1) * 512]
                    dsl = stash[:, (0 + j) * 512:(1 + j) * 512]
                    asl = stash[:, (2 + j) * 512:(3 + j) * 512]
                    W.wait("scalar", "mm", mm_r[j])
                    W.opn("scalar", lambda e, lo=lo, o=o: e.activation(
                        pool[:, 2566:2566 + TA], psB[:, lo:lo + TA], AF.Tanh,
                        scale=0.5, bias=c_hba[:, o:o + 1]))
                    W.opn("scalar", lambda e, o=o, asl=asl: e.activation(
                        asl, pool[:, 2566:2566 + TA], AF.Exp,
                        scale=c_c1[:, o:o + 1], bias=c_c1[:, o:o + 1]))
                    s_a2 = W.op("scalar", "sc", lambda e, o=o, zsl=zsl: e.activation(
                        zsl, pool[:, 2566:2566 + TA], AF.Exp,
                        scale=c_c2[:, o:o + 1], bias=c_c2[:, o:o + 1]))
                    mark_read("B", lo, lo + 512, "sc", s_a2)
                    W.wait("scalar", "mm", mm_i[j])
                    s_ti = W.op("scalar", "sc", lambda e, lo=lo, o=o, dsl=dsl:
                                e.activation(dsl, psC[:, lo:lo + TA], AF.Tanh,
                                             scale=0.5, bias=c_hbi[:, o:o + 1]))
                    mark_read("C", lo, lo + 512, "sc", s_ti)
                    W.wait("vector", "sc", s_ti)
                    W.opn("vector", lambda e, zsl=zsl: e.tensor_scalar(
                        zsl, zsl, -1.0, 1.0, ALU.mult, ALU.add))
                    W.opn("vector", lambda e, zsl=zsl: e.tensor_scalar(
                        zsl, zsl, 1e-12, None, ALU.max))
                    W.opn("vector", lambda e, dsl=dsl, o=o: e.tensor_tensor(
                        pool[:, 3078:3078 + TA], dsl,
                        bigA[:, o * 512:o * 512 + TA], ALU.mult))
                    W.op("vector", "v", lambda e, dsl=dsl, o=o: e.tensor_tensor(
                        dsl, pool[:, 3078:3078 + TA],
                        bigA[:, o * 512:o * 512 + TA], ALU.add))
                v_ph1 = W.cnt["v"]
                W.wait("scalar", "v", v_ph1)
                for j in range(2):
                    o = grp * 2 + j
                    zsl = stashz[:, j * 512:(j + 1) * 512]
                    dsl = stash[:, (0 + j) * 512:(1 + j) * 512]
                    asl = stash[:, (2 + j) * 512:(3 + j) * 512]
                    bsq = pool[:, 2566:2566 + TA] if j == 0 else pool[:, 3078:3078 + TA]
                    s_b = W.op("scalar", "sc", lambda e, zsl=zsl, bsq=bsq:
                               e.activation(bsq, zsl, AF.Sqrt))
                    W.wait("vector", "sc", s_b)
                    W.opn("vector", lambda e, bsq=bsq, dsl=dsl: e.scalar_tensor_tensor(
                        pool[:, 3590:3590 + TA], bsq, 0.5, dsl, ALU.mult, ALU.mult))
                    rsl = (o % 2) * 512
                    key = (t, o - 2) if o >= 2 else (t - 1, KH - 2 + o)
                    if key in spill_st:
                        W.wait("vector", "st", spill_st[key])
                    W.opn("vector", lambda e, asl=asl, o=o: e.tensor_tensor_scan(
                        pool[:, 4102:4102 + TA], asl, pool[:, 3590:3590 + TA],
                        carry[:, o:o + 1], ALU.mult, ALU.add))
                    W.opn("vector", lambda e, asl=asl, o=o: e.tensor_tensor_scan(
                        pool[:, 4614:4614 + TA], asl, asl,
                        prefp[:, o:o + 1], ALU.mult, ALU.min))
                    W.opn("vector", lambda e, o=o: e.tensor_copy(
                        carry[:, o:o + 1], pool[:, 4102 + TA - 1:4102 + TA]))
                    W.opn("vector", lambda e, o=o: e.tensor_copy(
                        prefp[:, o:o + 1], pool[:, 4614 + TA - 1:4614 + TA]))
                    W.opn("vector", lambda e, rsl=rsl: e.tensor_copy(
                        bigB[:, rsl:rsl + TA], pool[:, 4102:4102 + TA]))
                    v_sp = W.op("vector", "v", lambda e, rsl=rsl: e.tensor_copy(
                        bigB[:, 1024 + rsl:1024 + rsl + TA], pool[:, 4614:4614 + TA]))
                    W.wait("scalar", "v", v_sp)
                    W.dma("scalar", "st",
                          rl_sc[o * 128:(o + 1) * 128, t * TA:(t + 1) * TA],
                          bigB[:, rsl:rsl + TA])
                    spill_st[(t, o)] = W.dma(
                        "scalar", "st",
                        ar_sc[o * 128:(o + 1) * 128, t * TA:(t + 1) * TA],
                        bigB[:, 1024 + rsl:1024 + rsl + TA])
                grp_v[(t, grp)] = W.cnt["v"]
            mm_wawi[t] = W.cnt["mm"]

        W.wait("scalar", "v", W.cnt["v"])
        c_carry = W.dma("scalar", "st", cc_in, carry[:, 0:KH])
        W.wait("gpsimd", "st", c_carry)
        W.cnt["cc"] += 1
        cc1 = W.cnt["cc"]
        W.emit("gpsimd", lambda e: e.collective_compute(
            "AllGather", ALU.bypass, replica_groups=RG8,
            ins=[cc_in], outs=[cc_out]).then_inc(W.handles["cc"], 1))
        W.wait("sync", "cc", cc1)
        W.wait("sync", "mm", W.cnt["mm"])
        W.wait("sync", "v", W.cnt["v"])
        gath = xt[0:8, 0:HID]
        c_gath = W.dma("sync", "x", gath, cc_out)
        W.wait("tensor", "x", c_gath)
        sel_parts = (("A", 0), ("A", 512), ("B", 0), ("B", 512), ("C", 0))
        for hp, (tn, lo) in enumerate(sel_parts):
            pe_guard(tn, lo, lo + 512)
            mm(tn, lo, lo + 512, sel_sb[0:8, 0:1],
               gath[0:8, hp * 512:(hp + 1) * 512], True, True,
               inc=(hp == len(sel_parts) - 1), p0=0, p1=1)
        sel_mm = W.cnt["mm"]
        W.wait("vector", "mm", sel_mm)
        hrow = xt[0:1, 4096:4096 + HID]
        v_h = 0
        for hp, (tn, lo) in enumerate(sel_parts):
            v_h = W.op("vector", "v", lambda e, t=PST[tn], lo=lo, hp=hp:
                       e.tensor_copy(hrow[0:1, hp * 512:(hp + 1) * 512],
                                     t[0:1, lo:lo + 512]))
            mark_read(tn, lo, lo + 512, "v", v_h)
        W.wait("scalar", "v", v_h)
        c_hd = W.dma("scalar", "st", hsel_d, hrow)
        W.wait("sync", "st", c_hd)
        hsel_x = W.dma("sync", "x", hsel[:, :],
                       hsel_d.rearrange("a (p k) -> (a p) k", k=KH))

        v_a2 = [0] * (NTA + 1)
        rlld_v = {}
        rlld_x = {}
        lr_v = {}
        xmid_st = {}

        def a2_ln_parts(t):
            par = (t % 2) * 512
            if t == 0:
                W.wait("sync", "st", c_hd)
                W.wait("sync", "mm", sel_mm)
                W.wait("sync", "v", W.cnt["v"])
            else:
                W.wait("sync", "v", v_a2[t - 1] if t >= 1 else 0)
                W.wait("sync", "st", W.cnt["st"])
            xc = 0
            for k in range(KD):
                xc = W.dma("sync", "x", xt[:, k * 1024 + par:k * 1024 + par + TA],
                           xin[128 * k:128 * (k + 1), t * TA:(t + 1) * TA])
            c_rows = W.dma("sync", "x", rows[0:1, 0:1024], rows_d[t])
            pe_guard("D", 0, 1024)
            W.wait("tensor", "x", c_rows)
            mm("D", 0, TA, ones_k1[0:1, :], rows[0:1, 0:TA], True, True)
            bc = mm("D", 512, 512 + TA, ones_k1[0:1, :], rows[0:1, 512:512 + TA],
                    True, True, inc=True)
            ln_state = {"v": 0, "prev": {}, "bc": bc, "xc": xc}

            def ln_chunk(k):
                xck = xt[:, k * 1024 + par:k * 1024 + par + TA]
                t0 = pv(5120 + (k % 2) * 512, TA)
                if k == 0:
                    W.wait("vector", "mm", ln_state["bc"])
                    W.wait("vector", "x", ln_state["xc"])
                W.opn("vector", lambda e, t0=t0, x=xck: e.tensor_tensor(
                    t0, x, psD[:, 0:TA], ALU.subtract),
                    dep=(None if k < 2 else [ln_state["prev"][k - 2]]))
                v_ln = W.opn("vector", lambda e, t0=t0, k=k: e.tensor_tensor(
                    xnt[:, k * 1024:k * 1024 + TA], t0, psD[:, 512:512 + TA],
                    ALU.mult), dep=W.cnt["v"])
                ln_state["prev"][k] = v_ln
                ln_state["v"] = v_ln
                if k == KD - 1:
                    mark_read("D", 0, 1024, "v", v_ln)
            return ln_state, [lambda k=k: ln_chunk(k) for k in range(KD)]

        W.vbar()
        ln_state, ln_q = a2_ln_parts(0)
        for f in ln_q:
            f()
        ln_q = []

        for t in range(NTA):
            par = (t % 2) * 512
            v_ln = ln_state["v"]

            WLPS = (("A", 0), ("A", 512), ("B", 0), ("B", 512))

            def wl_stageA(o):
                buf, wc = load_strip(wl_t[o], C)
                W.wait("tensor", "w", wc)
                W.wait("tensor", "v", v_ln)
                tn, lo = WLPS[o % 4]
                pe_guard(tn, lo, lo + TA)
                g_mm = None
                for k in range(KD):
                    g_mm = mm(tn, lo, lo + TA, wv(buf, k),
                              xnt[:, k * 1024:k * 1024 + TA],
                              k == 0, k == KD - 1, inc=(k == KD - 1))
                strips_done(1)
                uh = (o % 2) * 512
                W.wait("scalar", "mm", g_mm)
                lr_m2 = lr_v.get((t, o - 2)) if o >= 2 else (
                    lr_v.get((t - 1, KH - 2 + o)) if t > 0 else None)
                if lr_m2:
                    W.wait("scalar", "v", lr_m2)
                W.opn("scalar", lambda e, uh=uh, tn=tn, lo=lo, o=o: e.activation(
                    pool[:, uh:uh + TA], PST[tn][:, lo:lo + TA], AF.Identity,
                    bias=c_bl[:, o:o + 1]))
                s_u = W.op("scalar", "sc", lambda e, uh=uh: e.activation(
                    pool[:, 1024 + uh:1024 + uh + TA], pool[:, uh:uh + TA],
                    AF.Square))
                mark_read(tn, lo, lo + 512, "sc", s_u)
                W.wait("vector", "sc", s_u)
                W.opn("vector", lambda e, uh=uh: e.tensor_tensor(
                    pool[:, 2048 + uh:2048 + uh + TA], pool[:, uh:uh + TA],
                    pool[:, 1024 + uh:1024 + uh + TA], ALU.mult), dep=[lr_m2])
                v_w = W.opn("vector", lambda e, uh=uh: e.scalar_tensor_tensor(
                    pool[:, 3072 + uh:3072 + uh + TA],
                    pool[:, 2048 + uh:2048 + uh + TA], 0.044715,
                    pool[:, uh:uh + TA], ALU.mult, ALU.add), dep=W.cnt["v"])
                return v_w

            def wl_stageB(o, v_w):
                uh = (o % 2) * 512
                W.wait("scalar", "v", v_w)
                s_th = W.op("scalar", "sc", lambda e, uh=uh: e.activation(
                    pool[:, 4096 + uh:4096 + uh + TA],
                    pool[:, 3072 + uh:3072 + uh + TA], AF.Tanh, scale=GC))
                W.wait("vector", "sc", s_th)
                W.opn("vector", lambda e, uh=uh: e.tensor_scalar(
                    pool[:, 4096 + uh:4096 + uh + TA],
                    pool[:, 4096 + uh:4096 + uh + TA],
                    0.5, 0.5, ALU.mult, ALU.add), dep=[v_w])
                W.opn("vector", lambda e, uh=uh: e.tensor_tensor(
                    pool[:, uh:uh + TA], pool[:, uh:uh + TA],
                    pool[:, 4096 + uh:4096 + uh + TA], ALU.mult),
                    dep=W.cnt["v"])
                W.wait("vector", "x", rlld_x[(t, o)])
                if t == 0 and o == 0:
                    W.wait("vector", "x", hsel_x)
                if t == 0 and o < 4:
                    W.wait("vector", "st", c_carry)
                rsl = (o % 2) * 512
                W.opn("vector", lambda e, rsl=rsl, o=o, uh=uh:
                      e.scalar_tensor_tensor(
                          pool[:, 1024 + uh:1024 + uh + TA],
                          bigA[:, 1024 + rsl:1024 + rsl + TA],
                          hsel[:, o:o + 1], bigA[:, rsl:rsl + TA],
                          ALU.mult, ALU.add), dep=W.cnt["v"] - 1)
                lr = W.opn("vector", lambda e, o=o, uh=uh: e.tensor_tensor(
                    bigB[:, o * 512:o * 512 + TA], pool[:, uh:uh + TA],
                    pool[:, 1024 + uh:1024 + uh + TA], ALU.mult),
                    dep=W.cnt["v"])
                rlld_v[(t, o)] = lr
                lr_v[(t, o)] = lr

            vws = {}
            for o in range(KH):
                rsl = (o % 2) * 512
                key = (t, o - 2) if o >= 2 else (t - 1, KH - 2 + o)
                if key in rlld_v:
                    W.wait("sync", "v", rlld_v[key])
                W.dma("sync", "x", bigA[:, rsl:rsl + TA],
                      rl_sc[o * 128:(o + 1) * 128, t * TA:(t + 1) * TA])
                rlld_x[(t, o)] = W.dma(
                    "sync", "x", bigA[:, 1024 + rsl:1024 + rsl + TA],
                    ar_sc[o * 128:(o + 1) * 128, t * TA:(t + 1) * TA])
                vws[o] = wl_stageA(o)
                if o >= 1:
                    wl_stageB(o - 1, vws[o - 1])
            wl_stageB(KH - 1, vws[KH - 1])
            v_lr_all = W.cnt["v"]

            if t + 1 < NTA:
                ln_state, ln_q = a2_ln_parts(t + 1)
            else:
                ln_state, ln_q = ln_state, []

            for o2 in range(KD):
                buf, wc = load_strip(wo_t[o2], HID)
                W.wait("tensor", "w", wc)
                W.wait("tensor", "v", v_lr_all)
                lo = (o2 % 2) * 512
                pe_guard("B", lo, lo + TA)
                for k in range(KH):
                    g_mm = mm("B", lo, lo + TA, wv(buf, k),
                              bigB[:, k * 512:k * 512 + TA],
                              k == 0, k == KH - 1, inc=(k == KH - 1))
                strips_done(1)
                W.wait("vector", "mm", g_mm)
                if (t, o2 - 2) in xmid_st:
                    W.wait("vector", "st", xmid_st[(t, o2 - 2)])
                v_r = W.opn("vector", lambda e, lo=lo, o2=o2, par=par:
                            e.scalar_tensor_tensor(
                                xt[:, o2 * 1024 + par:o2 * 1024 + par + TA],
                                psB[:, lo:lo + TA],
                                c_bo[:, o2:o2 + 1],
                                xt[:, o2 * 1024 + par:o2 * 1024 + par + TA],
                                ALU.add, ALU.add))
                mark_read("B", lo, lo + 512, "v", v_r)
                W.wait("scalar", "v", v_r)
                xmid_st[(t, o2)] = W.dma(
                    "scalar", "st", xmid[o2 * 128:(o2 + 1) * 128, t * TA:(t + 1) * TA],
                    xt[:, o2 * 1024 + par:o2 * 1024 + par + TA])
                for _ in range(2):
                    if ln_q:
                        ln_q.pop(0)()
            v_a2[t] = W.cnt["v"]
        v_ffn_prev = 0
        for jt in range(NTF):
            W.vbar()
            W.wait("sync", "st", W.cnt["st"])
            W.wait("sync", "v", v_ffn_prev if jt > 0 else v_a2[NTA - 1])
            W.wait("sync", "mm", W.cnt["mm"])
            xc = 0
            for k in range(KD):
                xc = W.dma("sync", "x", xt[:, k * 1024:(k + 1) * 1024],
                           xmid[128 * k:128 * (k + 1), jt * TF:(jt + 1) * TF])
            W.barrier("scalar", ("mm", "v"))
            W.barrier("tensor", ("sc", "v"))
            W.barrier("vector", ("mm", "sc"))
            W.wait("scalar", "x", xc)
            W.wait("tensor", "x", xc)
            pe_guard("A", 0, 1024)
            pe_guard("B", 0, 1024)
            last_sq = {}
            for k in range(KD):
                xck = xt[:, k * 1024:(k + 1) * 1024]
                xq = pv(4096 + (k % 2) * 1024, TF)
                if k >= 2:
                    W.wait("scalar", "mm", last_sq[k - 2])
                csq = W.op("scalar", "sc", lambda e, o=xq, i=xck: e.activation(
                    o, i, AF.Square))
                W.wait("tensor", "sc", csq)
                for hh in range(2):
                    mm("A", hh * 512, (hh + 1) * 512, ones_col[:, 0:1],
                       xck[:, hh * 512:(hh + 1) * 512], k == 0, k == KD - 1,
                       p0=0, p1=1)
                    last_sq[k] = mm("B", hh * 512, (hh + 1) * 512, ones_col[:, 0:1],
                                    xq[:, hh * 512:(hh + 1) * 512], k == 0,
                                    k == KD - 1, inc=(hh == 1), p0=0, p1=1)
            stats_done = W.cnt["mm"]
            mean_r = rows[0:1, 0:TF]
            rstd_r = rows[0:1, TF:2 * TF]
            W.wait("vector", "mm", stats_done)
            v_mean = W.op("vector", "v", lambda e: e.tensor_scalar(
                mean_r, psA[0:1, 0:TF], 1.0 / C, None, ALU.mult))
            W.wait("scalar", "v", v_mean)
            c_m2 = W.op("scalar", "sc", lambda e: e.activation(
                pool[0:1, 0:TF], mean_r, AF.Square))
            W.wait("vector", "sc", c_m2)
            v_var = W.op("vector", "v", lambda e: e.scalar_tensor_tensor(
                pool[0:1, 0:TF], psB[0:1, 0:TF], 1.0 / C, pool[0:1, 0:TF],
                ALU.mult, ALU.subtract))
            mark_read("A", 0, 1024, "v", v_var)
            mark_read("B", 0, 1024, "v", v_var)
            W.wait("scalar", "v", v_var)
            c_sd = W.op("scalar", "sc", lambda e: e.activation(
                pool[0:1, 0:TF], pool[0:1, 0:TF], AF.Sqrt, bias=1e-5))
            W.wait("vector", "sc", c_sd)
            v_rst = W.op("vector", "v", lambda e: e.reciprocal(
                rstd_r, pool[0:1, 0:TF]))
            pe_guard("C", 0, 1024)
            pe_guard("D", 0, 1024)
            W.wait("tensor", "v", v_rst)
            for hh in range(2):
                mm("C", hh * 512, (hh + 1) * 512, ones_k1[0:1, :],
                   mean_r[0:1, hh * 512:(hh + 1) * 512], True, True)
                bc = mm("D", hh * 512, (hh + 1) * 512, ones_k1[0:1, :],
                        rstd_r[0:1, hh * 512:(hh + 1) * 512], True, True,
                        inc=(hh == 1))
            W.wait("vector", "mm", bc)
            pe_guard("A", 0, 1024)
            rt_mm = {}
            v_xn = 0
            for k in range(KD):
                xnf = pv((k % 2) * 1024, TF)
                xck = xt[:, k * 1024:(k + 1) * 1024]
                if k >= 2:
                    W.wait("vector", "mm", rt_mm[k - 2])
                W.opn("vector", lambda e, xnf=xnf, x=xck: e.tensor_tensor(
                    xnf, x, psC[:, 0:TF], ALU.subtract))
                W.opn("vector", lambda e, xnf=xnf: e.tensor_tensor(
                    xnf, xnf, psD[:, 0:TF], ALU.mult))
                v_xn = W.op("vector", "v", lambda e, xnf=xnf, k=k: e.tensor_copy(
                    xnt[:, k * 1024:(k + 1) * 1024], xnf))
                W.wait("tensor", "v", v_xn)
                for hh in range(2):
                    rt_mm[k] = mm("A", hh * 512, (hh + 1) * 512,
                                  gs_sb[:, k * 9:(k + 1) * 9],
                                  xnf[:, hh * 512:(hh + 1) * 512],
                                  k == 0, k == KD - 1, inc=(hh == 1),
                                  p0=0, p1=9)
            mark_read("C", 0, 1024, "v", v_xn)
            mark_read("D", 0, 1024, "v", v_xn)
            rt_sb = pool[0:9, 4096:4096 + TF]
            W.wait("vector", "mm", rt_mm[KD - 1])
            v_rt = W.op("vector", "v", lambda e: e.tensor_copy(
                rt_sb, psA[0:9, 0:TF]))
            mark_read("A", 0, 1024, "v", v_rt)
            pe_guard("C", 0, 512)
            W.wait("tensor", "v", v_rt)
            tp_c = 0
            for g in range(8):
                W.cnt["mm"] += 1
                tp_c = W.cnt["mm"]
                W.emit("tensor", lambda e, g=g: e.transpose(
                    psC[:, g * 9:(g + 1) * 9], rt_sb[0:9, g * 128:(g + 1) * 128],
                    ident[0:9, 0:9]).then_inc(W.handles["mm"], 1))
            lg = pool[:, 5120:5192]
            cmb = pool[:, 5248:5320]
            W.wait("vector", "mm", tp_c)
            v_lg = W.op("vector", "v", lambda e: e.tensor_copy(lg, psC[:, 0:72]))
            mark_read("C", 0, 512, "v", v_lg)
            sm = pool[:, 5376:5377]
            nm = pool[:, 5377:5378]
            ex8 = pool[:, 5384:5392]
            pr8 = pool[:, 5392:5400]
            eq8 = pool[:, 5400:5408]
            m1c = pool[:, 5408:5409]
            m2c = pool[:, 5409:5410]
            v_cmb = 0
            for g in range(8):
                v_nm = W.op("vector", "v", lambda e, g=g: e.reduce_max(
                    nm, lg[:, g * 9:g * 9 + 8], axis=AX.X, negate=True))
                W.wait("scalar", "v", v_nm)
                s_ex = W.op("scalar", "sc", lambda e, g=g: e.activation(
                    ex8, lg[:, g * 9:g * 9 + 8], AF.Exp, bias=nm[:, 0:1]))
                s_gs = W.op("scalar", "sc", lambda e, g=g: e.activation(
                    nm, lg[:, g * 9 + 8:g * 9 + 9], AF.Tanh, scale=0.5))
                W.wait("vector", "sc", s_gs)
                W.opn("vector", lambda e: e.reduce_sum(sm, ex8, axis=AX.X))
                W.opn("vector", lambda e: e.reciprocal(sm, sm))
                W.opn("vector", lambda e: e.tensor_scalar(
                    pr8, ex8, sm[:, 0:1], None, ALU.mult))
                W.opn("vector", lambda e: e.reduce_max(m1c, pr8, axis=AX.X))
                W.opn("vector", lambda e: e.tensor_scalar(
                    eq8, pr8, m1c[:, 0:1], None, ALU.is_equal))
                W.opn("vector", lambda e: e.scalar_tensor_tensor(
                    eq8, eq8, -2.0, pr8, ALU.mult, ALU.add))
                W.opn("vector", lambda e: e.reduce_max(m2c, eq8, axis=AX.X))
                W.opn("vector", lambda e: e.tensor_scalar(
                    eq8, pr8, m2c[:, 0:1], None, ALU.is_ge))
                W.opn("vector", lambda e, g=g: e.tensor_tensor(
                    cmb[:, g * 9:g * 9 + 8], pr8, eq8, ALU.mult))
                v_cmb = W.op("vector", "v", lambda e, g=g: e.tensor_scalar(
                    cmb[:, g * 9 + 8:g * 9 + 9], nm, 0.5, 0.5, ALU.mult, ALU.add))
            pe_guard("B", 0, 512)
            W.wait("tensor", "v", v_cmb)
            W.cnt["mm"] += 1
            c_tp = W.cnt["mm"]
            W.emit("tensor", lambda e: e.transpose(
                psB[0:72, 0:128], cmb, ident[:, :]).then_inc(W.handles["mm"], 1))
            W.wait("vector", "mm", c_tp)
            cT = pool[0:72, 4096:4096 + 128]
            v_cT = W.op("vector", "v", lambda e: e.tensor_copy(cT, psB[0:72, 0:128]))
            mark_read("B", 0, 512, "v", v_cT)
            W.wait("scalar", "v", v_cT)
            c_cmb = W.dma("scalar", "st", comb_d, cT)

            crow = rows[0:1, 0:TF]
            comb_r = comb_d.rearrange("(g i) p -> i g p", i=9)
            h_v = {}
            acc_v = {}
            prev_cb_mm = 0
            for pr_i in range(6):
                W.wait("vector", "mm", W.cnt["mm"])
                if pr_i == 0:
                    W.vbar()
                for ei in range(2):
                    e_idx = pr_i * 2 + ei
                    row_i = e_idx if e_idx < 8 else 8
                    hbuf = bigA if ei == 0 else bigB
                    W.wait("sync", "st", c_cmb)
                    W.wait("sync", "mm", prev_cb_mm)
                    c_row = W.dma("sync", "x", crow, comb_r[row_i])
                    pe_guard("C", 0, 1024)
                    W.wait("tensor", "x", c_row)
                    mm("C", 0, 512, ones_k1[0:1, :], crow[0:1, 0:512], True, True)
                    cb_mm = mm("C", 512, 1024, ones_k1[0:1, :],
                               crow[0:1, 512:1024], True, True, inc=True)
                    prev_cb_mm = cb_mm
                    W.wait("vector", "mm", cb_mm)
                    cb_v = W.op("vector", "v", lambda e: e.tensor_copy(
                        pv(5120, TF), psC[:, 0:TF]))
                    mark_read("C", 0, 1024, "v", cb_v)
                    for o in range(KF):
                        b1, w1 = load_strip(fc1_t[e_idx, o], C)
                        b2, w2 = load_strip(fc2_t[e_idx, o], C)
                        W.wait("tensor", "w", w2)
                        W.wait("tensor", "v", v_xn)
                        t1 = "A" if o % 2 == 0 else "C"
                        t2 = "B" if o % 2 == 0 else "D"
                        pe_guard(t1, 0, 1024)
                        for k in range(KD):
                            for hh in range(2):
                                a_mm = mm(t1, hh * 512, (hh + 1) * 512,
                                          wv(b1, k),
                                          xnt[:, k * 1024 + hh * 512:
                                              k * 1024 + (hh + 1) * 512],
                                          k == 0, k == KD - 1,
                                          inc=(hh == 1 and k == KD - 1))
                        pe_guard(t2, 0, 1024)
                        for k in range(KD):
                            for hh in range(2):
                                b_mm = mm(t2, hh * 512, (hh + 1) * 512,
                                          wv(b2, k),
                                          xnt[:, k * 1024 + hh * 512:
                                              k * 1024 + (hh + 1) * 512],
                                          k == 0, k == KD - 1,
                                          inc=(hh == 1 and k == KD - 1))
                        strips_done(2)
                        hsl = 2048 + (o % 2) * 1024
                        W.wait("scalar", "mm", a_mm)
                        if (e_idx, o - 2) in h_v:
                            W.wait("scalar", "v", h_v[(e_idx, o - 2)])
                        s_si = W.op("scalar", "sc", lambda e, hsl=hsl, t1=t1:
                                    e.activation(pool[:, hsl:hsl + TF],
                                                 PST[t1][:, 0:TF], AF.Silu))
                        mark_read(t1, 0, 1024, "sc", s_si)
                        W.wait("vector", "sc", s_si)
                        W.wait("vector", "mm", b_mm)
                        dep0 = [h_v.get((e_idx, o - 2)), cb_v]
                        W.opn("vector", lambda e, hsl=hsl, t2=t2: e.tensor_tensor(
                            pool[:, hsl:hsl + TF], pool[:, hsl:hsl + TF],
                            PST[t2][:, 0:TF], ALU.mult), dep=dep0)
                        hv = W.opn("vector", lambda e, hsl=hsl, hb=hbuf, o=o:
                                   e.tensor_tensor(hb[:, o * 1024:(o + 1) * 1024],
                                                   pool[:, hsl:hsl + TF],
                                                   pv(5120, TF), ALU.mult),
                                   dep=W.cnt["v"])
                        mark_read(t2, 0, 1024, "v", hv)
                        h_v[(e_idx, o)] = hv
                v_h_all = W.cnt["v"]
                for o2 in range(KD):
                    bufs = []
                    for ei in range(2):
                        bufs.append(load_strip(prj_t[pr_i * 2 + ei, o2], FM))
                    W.wait("tensor", "w", bufs[1][1])
                    W.wait("tensor", "v", v_h_all)
                    tn = "C" if o2 % 2 == 0 else "D"
                    pe_guard(tn, 0, 1024)
                    p_mm = None
                    for ei in range(2):
                        hb = bigA if ei == 0 else bigB
                        bpr = bufs[ei][0]
                        for k in range(KF):
                            for hh in range(2):
                                p_mm = mm(tn, hh * 512, (hh + 1) * 512,
                                          wv(bpr, k),
                                          hb[:, k * 1024 + hh * 512:
                                             k * 1024 + (hh + 1) * 512],
                                          ei == 0 and k == 0,
                                          ei == 1 and k == KF - 1,
                                          inc=(ei == 1 and k == KF - 1 and hh == 1))
                    strips_done(2)
                    W.wait("vector", "mm", p_mm)
                    v_acc = W.opn("vector", lambda e, tn=tn, o2=o2:
                                  e.tensor_tensor(
                                      xt[:, o2 * 1024:(o2 + 1) * 1024],
                                      xt[:, o2 * 1024:(o2 + 1) * 1024],
                                      PST[tn][:, 0:TF], ALU.add),
                                  dep=[acc_v.get(o2 - 2), acc_v.get(o2)])
                    mark_read(tn, 0, 1024, "v", v_acc)
                    acc_v[o2] = v_acc
            for k in range(KD):
                W.wait("scalar", "v", acc_v[k])
                W.dma("scalar", "st", yout[128 * k:128 * (k + 1), jt * TF:(jt + 1) * TF],
                      xt[:, k * 1024:(k + 1) * 1024])
            v_ffn_prev = W.cnt["v"]

        if debug:
            W.barrier("scalar", ("mm", "sc", "v"))
            W.dma("scalar", "st", dbg_stash, stash[:, 0:3072])
            W.dma("scalar", "st", dbg_c[:, 0:KH], c_c1[:, :])
            W.dma("scalar", "st", dbg_c[:, KH:2 * KH], c_c2[:, :])
            W.dma("scalar", "st", dbg_pool, pool[:, :])
        W.wait("scalar", "st", W.cnt["st"])
        W.wait("sync", "st", W.cnt["st"])

        with ExitStack() as es2:
            for s in W.cnt:
                W.handles[s] = es2.enter_context(nc.semaphore(s))
            block = es2.enter_context(nc.Block())
            W.run(block)

    return nc


def _prep_inputs(kw):
    f32 = np.float32
    x = kw["L0_attn_x"]
    xT = np.ascontiguousarray(np.asarray(x, dtype=f32).transpose(0, 2, 1))

    def tile_w(w, kchunks, ochunks):
        r = np.asarray(w, dtype=f32).reshape(kchunks, 128, ochunks, 128)
        return np.ascontiguousarray(
            r.transpose(2, 1, 0, 3).reshape(ochunks, 128, kchunks * 128))

    wr = tile_w(kw["L0_attn_Wr"], KD, KH)
    wl = tile_w(kw["L0_attn_Wl"], KD, KH)
    wa = tile_w(kw["L0_attn_Wa"], KH, KH)
    wi = tile_w(kw["L0_attn_Wi"], KH, KH)
    wo = tile_w(kw["L0_attn_Wo"], KH, KD)

    fc1 = np.concatenate([np.asarray(kw["L0_ffn_expert_fc1"], f32),
                          np.asarray(kw["L0_ffn_shared_fc1"], f32).reshape(4, FM, C)],
                         axis=0)
    fc2 = np.concatenate([np.asarray(kw["L0_ffn_expert_fc2"], f32),
                          np.asarray(kw["L0_ffn_shared_fc2"], f32).reshape(4, FM, C)],
                         axis=0)
    prj = np.concatenate([np.asarray(kw["L0_ffn_expert_proj"], f32),
                          np.asarray(kw["L0_ffn_shared_proj"], f32)
                          .reshape(C, 4, FM).transpose(1, 0, 2)], axis=0)
    fc1_t = np.stack([tile_w(fc1[e].T, KD, KF) for e in range(NE)])
    fc2_t = np.stack([tile_w(fc2[e].T, KD, KF) for e in range(NE)])
    prj_t = np.stack([tile_w(prj[e].T, KF, KD) for e in range(NE)])

    gct = np.concatenate([np.asarray(kw["L0_ffn_gate_w"], f32),
                          np.asarray(kw["L0_ffn_shared_gate_w"], f32)], axis=0)
    gs = np.ascontiguousarray(
        gct.T.reshape(KD, 128, 9).transpose(1, 0, 2).reshape(128, KD * 9))

    def vecT(v, kchunks):
        return np.ascontiguousarray(np.asarray(v, f32).reshape(kchunks, 128).T)

    consts = dict(
        blT=vecT(kw["L0_attn_bl"], KH), brT=vecT(kw["L0_attn_br"], KH),
        cbT=vecT(kw["L0_attn_conv_b"], KH), baT=vecT(kw["L0_attn_ba"], KH),
        biT=vecT(kw["L0_attn_bi"], KH), lamT=vecT(kw["L0_attn_lam"], KH),
        boT=vecT(kw["L0_attn_bo"], KD),
        cwT=np.ascontiguousarray(np.concatenate(
            [vecT(np.asarray(kw["L0_attn_conv_w"], f32)[t, 0], KH)
             for t in range(4)], axis=1)),
        identT=np.eye(128, dtype=f32),
        gs_t=gs,
        fconst=np.tile(np.array([[-1.0, 1.0, 1e-5, 0.5, GC, 0.0, 2.0, -2.0]], f32), (128, 1)),
    )

    in_maps = []
    for c in range(8):
        b, h = c // 2, c % 2
        m = dict(xin=np.ascontiguousarray(xT[b, :, h * TC:(h + 1) * TC]),
                 wr_t=wr, wl_t=wl, wa_t=wa, wi_t=wi, wo_t=wo,
                 fc1_t=fc1_t, fc2_t=fc2_t, prj_t=prj_t)
        if h == 1:
            xpre = xT[b, :, h * TC - 3:h * TC]
        else:
            xpre = np.zeros((C, 3), f32)
        m["xpreT"] = np.ascontiguousarray(xpre.reshape(KD, 128, 3))
        m["maskT"] = np.full((128, 1), float(h), f32)
        sel = np.zeros((8, 1), f32)
        if h == 1:
            sel[c - 1, 0] = 1.0
        m["selT"] = sel
        m.update(consts)
        in_maps.append(m)
    return in_maps


def kernel(**kw):
    in_maps = _prep_inputs(kw)
    nc = build_program()
    res = run_bass_kernel_spmd(nc, in_maps, core_ids=list(range(8)))
    out = np.empty((4, 4096, 2048), np.float32)
    for c in range(8):
        b, h = c // 2, c % 2
        out[b, h * TC:(h + 1) * TC, :] = res.results[c]["yout"].T
    return out


# version 07
# speedup vs baseline: 1.0571x; 1.0571x over previous
import numpy as np
import concourse.bass as bass
import concourse.mybir as mybir
from concourse.bass_utils import run_bass_kernel_spmd
from contextlib import ExitStack

F32 = mybir.dt.float32
BF16 = mybir.dt.bfloat16
AF = mybir.ActivationFunctionType
ALU = mybir.AluOpType
AX = mybir.AxisListType

C = 2048
HID = 2560
KD = 16
KH = 20
KF = 11
FM = 1408
NE = 12
TC = 2048
TA = 512
NTA = 4
TF = 1024
NTF = 2
GC = 0.7978845608028654
ENGINES = ("sync", "gpsimd", "tensor", "vector", "scalar")
RG8 = [[0, 1, 2, 3, 4, 5, 6, 7]]


class B:

    def __init__(self):
        self.streams = {e: [] for e in ENGINES}
        self.cnt = {s: 0 for s in ("w", "x", "st", "mm", "sc", "v", "cc")}
        self.handles = {}
        self.last_wait = {e: {} for e in ENGINES}

    def emit(self, eng, fn):
        self.streams[eng].append(fn)

    def wait(self, eng, s, val):
        if val is None or val <= 0:
            return
        lw = self.last_wait[eng]
        if lw.get(s, -1) >= val:
            return
        lw[s] = val
        h = self.handles
        self.emit(eng, lambda e, s=s, v=val: e.wait_ge(h[s], v))

    def vbar(self):
        self.wait("vector", "v", self.cnt["v"])

    def barrier(self, eng, sems=("mm", "sc", "v")):
        for s in sems:
            self.wait(eng, s, self.cnt[s])

    def dma(self, eng, sname, out, in_):
        self.cnt[sname] += 16
        c = self.cnt[sname]
        h = self.handles
        self.emit(eng, lambda e, o=out, i=in_, s=sname: e.dma_start(out=o, in_=i)
                  .then_inc(h[s], 16))
        return c

    def op(self, eng, sname, fn, dep=None):
        if eng == "vector":
            assert sname == "v"
            return self.opn(eng, fn, dep=dep)
        self.cnt[sname] += 1
        c = self.cnt[sname]
        h = self.handles
        self.emit(eng, lambda e, fn=fn, s=sname: fn(e).then_inc(h[s], 1))
        return c

    def opn(self, eng, fn, dep=None):
        if eng == "vector":
            if dep is None:
                dep = self.cnt["v"]
            elif isinstance(dep, (list, tuple)):
                dep = max([d for d in dep if d] or [0])
            self.wait(eng, "v", dep)
            self.cnt["v"] += 1
            c = self.cnt["v"]
            h = self.handles
            self.emit(eng, lambda e, fn=fn: fn(e).then_inc(h["v"], 1))
            return c
        self.emit(eng, lambda e, fn=fn: fn(e))

    def run(self, block):
        for eng_name in ENGINES:
            ops = self.streams[eng_name]
            getattr(block, eng_name)(lambda e, ops=ops: [f(e) for f in ops])


def build_program(debug=False):
    nc = bass.Bass("TRN2", target_bir_lowering=False)
    dbg_kind = "ExternalOutput" if debug else "Internal"

    def din(name, shape):
        return nc.dram_tensor(name, list(shape), F32, kind="ExternalInput").ap()

    xin = din("xin", (C, TC))
    xpreT = din("xpreT", (KD, 128, 3))
    maskT = din("maskT", (128, 1))
    selT = din("selT", (8, 1))
    identT = din("identT", (128, 128))
    wr_t = din("wr_t", (KH, 128, C))
    wl_t = din("wl_t", (KH, 128, C))
    wa_t = din("wa_t", (KH, 128, HID))
    wi_t = din("wi_t", (KH, 128, HID))
    wo_t = din("wo_t", (KD, 128, HID))
    fc1_t = din("fc1_t", (NE, KF, 128, C))
    fc2_t = din("fc2_t", (NE, KF, 128, C))
    prj_t = din("prj_t", (NE, KD, 128, FM))
    gs_t = din("gs_t", (128, KD * 9))
    blT = din("blT", (128, KH))
    brT = din("brT", (128, KH))
    cbT = din("cbT", (128, KH))
    baT = din("baT", (128, KH))
    biT = din("biT", (128, KH))
    lamT = din("lamT", (128, KH))
    boT = din("boT", (128, KD))
    cwT = din("cwT", (128, 4 * KH))
    fconst = din("fconst", (128, 8))

    yout = nc.dram_tensor("yout", [C, TC], F32, kind="ExternalOutput").ap()

    xmid = nc.dram_tensor("xmid", [C, TC], F32, kind=dbg_kind).ap()
    rl_sc = nc.dram_tensor("rl_sc", [HID, TC], BF16, kind=dbg_kind).ap()
    ar_sc = nc.dram_tensor("ar_sc", [HID, TC], BF16, kind=dbg_kind).ap()
    rows_d = nc.dram_tensor("rows_d", [NTA, 1024], F32, kind=dbg_kind).ap()
    cc_in = nc.dram_tensor("cc_in", [1, HID], F32).ap()
    cc_out = nc.dram_tensor("cc_out", [8, HID], F32).ap()
    hsel_d = nc.dram_tensor("hsel_d", [1, HID], F32, kind=dbg_kind).ap()
    comb_d = nc.dram_tensor("comb_d", [72, 128], F32, kind=dbg_kind).ap()
    if debug:
        dbg_stash = nc.dram_tensor("dbg_stash", [128, 3072], F32, kind="ExternalOutput").ap()
        dbg_c = nc.dram_tensor("dbg_c", [128, 2 * KH], F32, kind="ExternalOutput").ap()
        dbg_pool = nc.dram_tensor("dbg_pool", [128, 6144], F32, kind="ExternalOutput").ap()

    es = ExitStack()

    def sbuf(name, shape, dtype=F32):
        return es.enter_context(nc.sbuf_tensor(name, list(shape), dtype))

    def psum(name, shape):
        return es.enter_context(nc.psum_tensor(name, list(shape), F32))

    with es:
        xt = sbuf("xt", (128, KD * 1024))
        xnt = sbuf("xnt", (128, KD * 1024), BF16)
        bigA = sbuf("bigA", (128, KF * 1024), BF16)
        bigB = sbuf("bigB", (128, KF * 1024), BF16)
        stash = sbuf("stash", (128, 4 * 512))
        stashz = sbuf("stashz", (128, 2 * 512), BF16)
        wbuf = sbuf("wbuf", (128, 4 * HID), BF16)
        pool = sbuf("pool", (128, 6144))
        rows = sbuf("rows", (1, 2048))
        gs_sb = sbuf("gs_sb", (128, KD * 9))
        ident = sbuf("ident", (128, 128))
        ones_k1 = sbuf("ones_k1", (1, 128))
        ones_col = sbuf("ones_col", (128, 1))
        c_bl = sbuf("c_bl", (128, KH))
        c_br = sbuf("c_br", (128, KH))
        c_cb = sbuf("c_cb", (128, KH))
        c_hba = sbuf("c_hba", (128, KH))
        c_hbi = sbuf("c_hbi", (128, KH))
        c_lam = sbuf("c_lam", (128, KH))
        c_c1 = sbuf("c_c1", (128, KH))
        c_c2 = sbuf("c_c2", (128, KH))
        c_bo = sbuf("c_bo", (128, KD))
        c_cw = sbuf("c_cw", (128, 4 * KH))
        c_mask = sbuf("c_mask", (128, 1))
        sel_sb = sbuf("sel_sb", (8, 1))
        carry = sbuf("carry", (128, KH))
        prefp = sbuf("prefp", (128, KH))
        rtail = sbuf("rtail", (128, 3 * KH))
        hsel = sbuf("hsel", (128, KH))
        fc_sb = sbuf("fc_sb", (128, 8))
        FVALS = (-1.0, 1.0, 1e-5, 0.5, GC, 0.0, 2.0, -2.0)
        for i, fv in enumerate(FVALS):
            nc.const_aps.register(F32, fv, fc_sb[:, i:i + 1])

        psA = psum("psA", (128, 1024))
        psB = psum("psB", (128, 1024))
        psC = psum("psC", (128, 1024))
        psD = psum("psD", (128, 1024))
        PST = {"A": psA, "B": psB, "C": psC, "D": psD}

        W = B()

        ps_reader = {}

        def regions(tn, lo, hi):
            out = []
            if lo < 512:
                out.append((tn, 0))
            if hi > 512:
                out.append((tn, 1))
            return out

        def pe_guard(tn, lo, hi):
            for r in regions(tn, lo, hi):
                if r in ps_reader:
                    s, cv = ps_reader[r]
                    W.wait("tensor", s, cv)

        def mark_read(tn, lo, hi, sem, cntv):
            for r in regions(tn, lo, hi):
                ps_reader[r] = (sem, cntv)

        def mm(tn, lo, hi, lhsT, rhs, start, stop, inc=False, p0=0, p1=128):
            t = PST[tn]
            if inc:
                W.cnt["mm"] += 1
                c = W.cnt["mm"]
                h = W.handles
                W.emit("tensor", lambda e, t=t, lo=lo, hi=hi, l=lhsT, r=rhs,
                       s1=start, s2=stop, p0=p0, p1=p1: e.matmul(
                           t[p0:p1, lo:hi], l, r, start=s1,
                           stop=s2).then_inc(h["mm"], 1))
                return c
            W.emit("tensor", lambda e, t=t, lo=lo, hi=hi, l=lhsT, r=rhs,
                   s1=start, s2=stop, p0=p0, p1=p1: e.matmul(
                       t[p0:p1, lo:hi], l, r, start=s1, stop=s2))
            return None

        strip_state = {"idx": 0, "free": []}

        def load_strip(src_ap, cols):
            idx = strip_state["idx"]
            strip_state["idx"] += 1
            buf = idx % 4
            if idx >= 4:
                W.wait("gpsimd", "mm", strip_state["free"][idx - 4])
            dst = wbuf[:, buf * HID:buf * HID + cols]
            W.cnt["w"] += 16
            wc = W.cnt["w"]
            h = W.handles
            W.emit("gpsimd", lambda e, d=dst, s=src_ap: e.dma_start(out=d, in_=s)
                   .then_inc(h["w"], 16))
            return buf, wc

        def strips_done(n):
            for _ in range(n):
                strip_state["free"].append(W.cnt["mm"])

        def wv(buf, k, n=128):
            return wbuf[:, buf * HID + k * 128:buf * HID + k * 128 + n]

        def pv(lo, n, p0=0, p1=128):
            return pool[p0:p1, lo:lo + n]

        for src, dst in ((blT, c_bl), (brT, c_br), (cbT, c_cb), (baT, c_hba),
                         (biT, c_hbi), (lamT, c_lam), (boT, c_bo), (cwT, c_cw),
                         (maskT, c_mask), (selT, sel_sb), (identT, ident),
                         (gs_t, gs_sb), (fconst, fc_sb)):
            W.dma("sync", "x", dst[:, :], src)
        for k in range(KD):
            W.dma("sync", "x", xt[:, k * 1024 + 512:k * 1024 + 515], xpreT[k])
        x_consts = W.cnt["x"]

        W.opn("vector", lambda e: e.memset(ones_k1[:, :], 1.0))
        W.opn("vector", lambda e: e.memset(ones_col[:, :], 1.0))
        W.opn("vector", lambda e: e.memset(carry[:, :], 0.0))
        W.opn("vector", lambda e: e.memset(prefp[:, :], 1.0))
        W.wait("vector", "x", x_consts)
        W.op("vector", "v", lambda e: e.tensor_scalar(
            c_hba[:, :], c_hba[:, :], 0.5, None, ALU.mult))
        W.op("vector", "v", lambda e: e.tensor_scalar(
            c_hbi[:, :], c_hbi[:, :], 0.5, None, ALU.mult))
        W.wait("scalar", "x", x_consts)
        W.op("scalar", "sc", lambda e: e.activation(
            c_c1[:, :], c_lam[:, :], AF.Exp, scale=-1.0))
        sc_sp = W.op("scalar", "sc", lambda e: e.activation(
            c_c1[:, :], c_c1[:, :], AF.Ln, bias=1.0))
        W.wait("vector", "sc", sc_sp)
        W.op("vector", "v", lambda e: e.tensor_scalar(
            c_c1[:, :], c_c1[:, :], -4.0, None, ALU.mult))
        W.op("vector", "v", lambda e: e.tensor_scalar(
            c_c2[:, :], c_c1[:, :], 2.0, None, ALU.mult))

        def ln_block(t0col, ncols, save_tile=None):
            W.barrier("scalar", ("mm", "v"))
            W.barrier("tensor", ("sc", "v"))
            W.barrier("vector", ("mm", "sc"))
            pe_guard("D", 0, 1024)
            last_sq = {}
            for k in range(KD):
                xck = xt[:, k * 1024 + t0col:k * 1024 + t0col + ncols]
                xq = pv(1542 + (k % 2) * 512, ncols)
                if k >= 2:
                    W.wait("scalar", "mm", last_sq[k - 2])
                csq = W.op("scalar", "sc", lambda e, o=xq, i=xck: e.activation(
                    o, i, AF.Square))
                W.wait("tensor", "sc", csq)
                mm("D", 0, ncols, ones_col[:, 0:1], xck, k == 0, k == KD - 1,
                   p0=0, p1=1)
                last_sq[k] = mm("D", 512, 512 + ncols, ones_col[:, 0:1], xq,
                                k == 0, k == KD - 1, inc=True, p0=0, p1=1)
            stats_done = W.cnt["mm"]
            mean_r = rows[0:1, 0:ncols]
            rstd_r = rows[0:1, 512:512 + ncols]
            m2_r = rows[0:1, 1024:1024 + ncols]
            var_r = rows[0:1, 1536:1536 + ncols]
            W.wait("vector", "mm", stats_done)
            v_mean = W.op("vector", "v", lambda e: e.tensor_scalar(
                mean_r, psD[0:1, 0:ncols], 1.0 / C, None, ALU.mult))
            W.wait("scalar", "v", v_mean)
            c_m2 = W.op("scalar", "sc", lambda e: e.activation(m2_r, mean_r, AF.Square))
            W.wait("vector", "sc", c_m2)
            v_var = W.op("vector", "v", lambda e: e.scalar_tensor_tensor(
                var_r, psD[0:1, 512:512 + ncols], 1.0 / C, m2_r,
                ALU.mult, ALU.subtract))
            W.wait("scalar", "v", v_var)
            c_sd = W.op("scalar", "sc", lambda e: e.activation(
                var_r, var_r, AF.Sqrt, bias=1e-5))
            W.wait("vector", "sc", c_sd)
            v_rst = W.op("vector", "v", lambda e: e.reciprocal(rstd_r, var_r))
            mark_read("D", 0, 1024, "v", v_var)
            if save_tile is not None:
                W.wait("scalar", "v", v_rst)
                W.dma("scalar", "st", rows_d[save_tile], rows[0:1, 0:1024])
            pe_guard("D", 0, 1024)
            W.wait("tensor", "v", v_rst)
            mm("D", 0, ncols, ones_k1[0:1, :], mean_r, True, True)
            bc = mm("D", 512, 512 + ncols, ones_k1[0:1, :], rstd_r, True, True,
                    inc=True)
            W.wait("vector", "mm", bc)
            v_last = 0
            lnp = {}
            for k in range(KD):
                xck = xt[:, k * 1024 + t0col:k * 1024 + t0col + ncols]
                t0 = pv(1030 + (k % 2) * 512, ncols)
                W.opn("vector", lambda e, t0=t0, x=xck: e.tensor_tensor(
                    t0, x, psD[:, 0:ncols], ALU.subtract),
                    dep=(None if k < 2 else [lnp[k - 2]]))
                v_last = W.opn("vector", lambda e, t0=t0, k=k: e.tensor_tensor(
                    xnt[:, k * 1024 + t0col:k * 1024 + t0col + ncols], t0,
                    psD[:, 512:512 + ncols], ALU.mult), dep=W.cnt["v"])
                lnp[k] = v_last
            mark_read("D", 0, 1024, "v", v_last)
            return v_last

        W.wait("scalar", "x", x_consts)
        W.wait("vector", "x", x_consts)
        W.wait("tensor", "x", x_consts)
        v_xnpre = ln_block(512, 3)

        spill_st = {}
        conv_v = {}
        mm_stats = [0] * NTA
        v_ln1 = [0] * NTA
        mm_wawi = [0] * NTA
        grp_v = {}

        def a1_load_ln(t):
            W.wait("sync", "v", v_ln1[t - 1] if t > 0 else 0)
            W.wait("sync", "mm", mm_stats[t - 1] if t > 0 else 0)
            xc = 0
            for k in range(KD):
                xc = W.dma("sync", "x", xt[:, k * 1024:k * 1024 + TA],
                           xin[128 * k:128 * (k + 1), t * TA:(t + 1) * TA])
            W.wait("scalar", "x", xc)
            W.wait("tensor", "x", xc)
            W.wait("vector", "x", xc)
            v_ln1[t] = ln_block(0, TA, save_tile=t)
            mm_stats[t] = W.cnt["mm"]

        a1_load_ln(0)
        for t in range(NTA):
            for o in range(KH):
                buf, wc = load_strip(wr_t[o], C)
                W.wait("tensor", "w", wc)
                W.wait("tensor", "v", v_ln1[t])
                lo = (o % 2) * 512
                pe_guard("A", lo, lo + TA)
                for k in range(KD):
                    g_mm = mm("A", lo, lo + TA, wv(buf, k),
                              xnt[:, k * 1024:k * 1024 + TA],
                              k == 0, k == KD - 1, inc=(k == KD - 1))
                p_mm = None
                if t == 0:
                    W.wait("tensor", "v", v_xnpre)
                    pe_guard("B", 512, 1024)
                    for k in range(KD):
                        p_mm = mm("B", 512, 515, wv(buf, k),
                                  xnt[:, k * 1024 + 512:k * 1024 + 515],
                                  k == 0, k == KD - 1, inc=(k == KD - 1))
                strips_done(1)
                rh = (o % 2) * 515
                W.wait("scalar", "mm", g_mm)
                if o >= 2:
                    W.wait("scalar", "v", conv_v[(t, o - 2)])
                e_sc = W.op("scalar", "sc", lambda e, rh=rh, lo=lo, o=o: e.activation(
                    pool[:, rh + 3:rh + 3 + TA], psA[:, lo:lo + TA], AF.Identity,
                    bias=c_br[:, o:o + 1]))
                mark_read("A", lo, lo + 512, "sc", e_sc)
                if t == 0:
                    W.wait("scalar", "mm", p_mm)
                    e_pre = W.op("scalar", "sc", lambda e, o=o: e.activation(
                        pool[:, 2566:2569], psB[:, 512:515], AF.Identity,
                        bias=c_br[:, o:o + 1]))
                    mark_read("B", 512, 1024, "sc", e_pre)
                    W.wait("vector", "sc", e_pre)
                    W.op("vector", "v", lambda e, o=o: e.tensor_scalar(
                        rtail[:, 3 * o:3 * o + 3], pool[:, 2566:2569],
                        c_mask[:, 0:1], None, ALU.mult))
                W.wait("vector", "sc", e_sc)
                W.opn("vector", lambda e, rh=rh, o=o: e.tensor_copy(
                    pool[:, rh:rh + 3], rtail[:, 3 * o:3 * o + 3]))
                W.opn("vector", lambda e, rh=rh, o=o: e.tensor_scalar(
                    pool[:, 1030:1030 + TA], pool[:, rh:rh + TA],
                    c_cw[:, o:o + 1], c_cb[:, o:o + 1], ALU.mult, ALU.add))
                for tp in range(1, 4):
                    W.opn("vector", lambda e, rh=rh, o=o, tp=tp: e.scalar_tensor_tensor(
                        pool[:, 1030:1030 + TA], pool[:, rh + tp:rh + tp + TA],
                        c_cw[:, tp * KH + o:tp * KH + o + 1],
                        pool[:, 1030:1030 + TA], ALU.mult, ALU.add))
                W.opn("vector", lambda e, rh=rh, o=o: e.tensor_copy(
                    rtail[:, 3 * o:3 * o + 3], pool[:, rh + TA:rh + TA + 3]))
                if t > 0:
                    W.wait("vector", "mm", mm_wawi[t - 1])
                conv_v[(t, o)] = W.op("vector", "v", lambda e, o=o: e.tensor_copy(
                    bigA[:, o * 512:(o + 1) * 512], pool[:, 1030:1030 + TA]))
            v_conv_all = W.cnt["v"]
            if t + 1 < NTA:
                W.vbar()
                a1_load_ln(t + 1)

            for grp in range(KH // 2):
                if grp == 0:
                    W.wait("scalar", "v", v_conv_all)
                else:
                    W.wait("scalar", "v", grp_v[(t, grp - 1)])
                mm_r = {}
                mm_i = {}
                for j in range(2):
                    o = grp * 2 + j
                    bufA, wA = load_strip(wa_t[o], HID)
                    bufI, wI = load_strip(wi_t[o], HID)
                    lo = (o % 2) * 512
                    W.wait("tensor", "w", wI)
                    W.wait("tensor", "v", v_conv_all)
                    pe_guard("B", lo, lo + TA)
                    for k in range(KH):
                        mm_r[j] = mm("B", lo, lo + TA, wv(bufA, k),
                                     bigA[:, k * 512:k * 512 + TA],
                                     k == 0, k == KH - 1, inc=(k == KH - 1))
                    pe_guard("C", lo, lo + TA)
                    for k in range(KH):
                        mm_i[j] = mm("C", lo, lo + TA, wv(bufI, k),
                                     bigA[:, k * 512:k * 512 + TA],
                                     k == 0, k == KH - 1, inc=(k == KH - 1))
                    strips_done(2)
                for j in range(2):
                    o = grp * 2 + j
                    lo = (o % 2) * 512
                    zsl = stashz[:, j * 512:(j + 1) * 512]
                    dsl = stash[:, (0 + j) * 512:(1 + j) * 512]
                    asl = stash[:, (2 + j) * 512:(3 + j) * 512]
                    W.wait("scalar", "mm", mm_r[j])
                    W.opn("scalar", lambda e, lo=lo, o=o: e.activation(
                        pool[:, 2566:2566 + TA], psB[:, lo:lo + TA], AF.Tanh,
                        scale=0.5, bias=c_hba[:, o:o + 1]))
                    W.opn("scalar", lambda e, o=o, asl=asl: e.activation(
                        asl, pool[:, 2566:2566 + TA], AF.Exp,
                        scale=c_c1[:, o:o + 1], bias=c_c1[:, o:o + 1]))
                    s_a2 = W.op("scalar", "sc", lambda e, o=o, zsl=zsl: e.activation(
                        zsl, pool[:, 2566:2566 + TA], AF.Exp,
                        scale=c_c2[:, o:o + 1], bias=c_c2[:, o:o + 1]))
                    mark_read("B", lo, lo + 512, "sc", s_a2)
                    W.wait("scalar", "mm", mm_i[j])
                    s_ti = W.op("scalar", "sc", lambda e, lo=lo, o=o, dsl=dsl:
                                e.activation(dsl, psC[:, lo:lo + TA], AF.Tanh,
                                             scale=0.5, bias=c_hbi[:, o:o + 1]))
                    mark_read("C", lo, lo + 512, "sc", s_ti)
                    W.wait("vector", "sc", s_ti)
                    W.opn("vector", lambda e, zsl=zsl: e.tensor_scalar(
                        zsl, zsl, -1.0, 1.0, ALU.mult, ALU.add))
                    W.opn("vector", lambda e, zsl=zsl: e.tensor_scalar(
                        zsl, zsl, 1e-12, None, ALU.max))
                    W.opn("vector", lambda e, dsl=dsl, o=o: e.tensor_tensor(
                        pool[:, 3078:3078 + TA], dsl,
                        bigA[:, o * 512:o * 512 + TA], ALU.mult))
                    W.op("vector", "v", lambda e, dsl=dsl, o=o: e.tensor_tensor(
                        dsl, pool[:, 3078:3078 + TA],
                        bigA[:, o * 512:o * 512 + TA], ALU.add))
                v_ph1 = W.cnt["v"]
                W.wait("scalar", "v", v_ph1)
                for j in range(2):
                    o = grp * 2 + j
                    zsl = stashz[:, j * 512:(j + 1) * 512]
                    dsl = stash[:, (0 + j) * 512:(1 + j) * 512]
                    asl = stash[:, (2 + j) * 512:(3 + j) * 512]
                    bsq = pool[:, 2566:2566 + TA] if j == 0 else pool[:, 3078:3078 + TA]
                    s_b = W.op("scalar", "sc", lambda e, zsl=zsl, bsq=bsq:
                               e.activation(bsq, zsl, AF.Sqrt))
                    W.wait("vector", "sc", s_b)
                    W.opn("vector", lambda e, bsq=bsq, dsl=dsl: e.scalar_tensor_tensor(
                        pool[:, 3590:3590 + TA], bsq, 0.5, dsl, ALU.mult, ALU.mult))
                    rsl = (o % 2) * 512
                    key = (t, o - 2) if o >= 2 else (t - 1, KH - 2 + o)
                    if key in spill_st:
                        W.wait("vector", "st", spill_st[key])
                    W.opn("vector", lambda e, asl=asl, o=o: e.tensor_tensor_scan(
                        pool[:, 4102:4102 + TA], asl, pool[:, 3590:3590 + TA],
                        carry[:, o:o + 1], ALU.mult, ALU.add))
                    W.opn("vector", lambda e, asl=asl, o=o: e.tensor_tensor_scan(
                        pool[:, 4614:4614 + TA], asl, asl,
                        prefp[:, o:o + 1], ALU.mult, ALU.min))
                    W.opn("vector", lambda e, o=o: e.tensor_copy(
                        carry[:, o:o + 1], pool[:, 4102 + TA - 1:4102 + TA]))
                    W.opn("vector", lambda e, o=o: e.tensor_copy(
                        prefp[:, o:o + 1], pool[:, 4614 + TA - 1:4614 + TA]))
                    W.opn("vector", lambda e, rsl=rsl: e.tensor_copy(
                        bigB[:, rsl:rsl + TA], pool[:, 4102:4102 + TA]))
                    v_sp = W.op("vector", "v", lambda e, rsl=rsl: e.tensor_copy(
                        bigB[:, 1024 + rsl:1024 + rsl + TA], pool[:, 4614:4614 + TA]))
                    W.wait("scalar", "v", v_sp)
                    W.dma("scalar", "st",
                          rl_sc[o * 128:(o + 1) * 128, t * TA:(t + 1) * TA],
                          bigB[:, rsl:rsl + TA])
                    spill_st[(t, o)] = W.dma(
                        "scalar", "st",
                        ar_sc[o * 128:(o + 1) * 128, t * TA:(t + 1) * TA],
                        bigB[:, 1024 + rsl:1024 + rsl + TA])
                grp_v[(t, grp)] = W.cnt["v"]
            mm_wawi[t] = W.cnt["mm"]

        W.wait("scalar", "v", W.cnt["v"])
        c_carry = W.dma("scalar", "st", cc_in, carry[:, 0:KH])
        W.wait("gpsimd", "st", c_carry)
        W.cnt["cc"] += 1
        cc1 = W.cnt["cc"]
        W.emit("gpsimd", lambda e: e.collective_compute(
            "AllGather", ALU.bypass, replica_groups=RG8,
            ins=[cc_in], outs=[cc_out]).then_inc(W.handles["cc"], 1))
        W.wait("sync", "cc", cc1)
        W.wait("sync", "mm", W.cnt["mm"])
        W.wait("sync", "v", W.cnt["v"])
        gath = xt[0:8, 0:HID]
        c_gath = W.dma("sync", "x", gath, cc_out)
        W.wait("tensor", "x", c_gath)
        sel_parts = (("A", 0), ("A", 512), ("B", 0), ("B", 512), ("C", 0))
        for hp, (tn, lo) in enumerate(sel_parts):
            pe_guard(tn, lo, lo + 512)
            mm(tn, lo, lo + 512, sel_sb[0:8, 0:1],
               gath[0:8, hp * 512:(hp + 1) * 512], True, True,
               inc=(hp == len(sel_parts) - 1), p0=0, p1=1)
        sel_mm = W.cnt["mm"]
        W.wait("vector", "mm", sel_mm)
        hrow = xt[0:1, 4096:4096 + HID]
        v_h = 0
        for hp, (tn, lo) in enumerate(sel_parts):
            v_h = W.op("vector", "v", lambda e, t=PST[tn], lo=lo, hp=hp:
                       e.tensor_copy(hrow[0:1, hp * 512:(hp + 1) * 512],
                                     t[0:1, lo:lo + 512]))
            mark_read(tn, lo, lo + 512, "v", v_h)
        W.wait("scalar", "v", v_h)
        c_hd = W.dma("scalar", "st", hsel_d, hrow)
        W.wait("sync", "st", c_hd)
        hsel_x = W.dma("sync", "x", hsel[:, :],
                       hsel_d.rearrange("a (p k) -> (a p) k", k=KH))

        v_a2 = [0] * (NTA + 1)
        rlld_v = {}
        rlld_x = {}
        lr_v = {}
        xmid_st = {}

        def a2_ln_parts(t):
            par = (t % 2) * 512
            if t == 0:
                W.wait("sync", "st", c_hd)
                W.wait("sync", "mm", sel_mm)
            W.wait("sync", "v", v_a2[t - 1] if t >= 1 else W.cnt["v"])
            xc = 0
            for k in range(KD):
                xc = W.dma("sync", "x", xt[:, k * 1024 + par:k * 1024 + par + TA],
                           xin[128 * k:128 * (k + 1), t * TA:(t + 1) * TA])
            c_rows = W.dma("sync", "x", rows[0:1, 0:1024], rows_d[t])
            pe_guard("D", 0, 1024)
            W.wait("tensor", "x", c_rows)
            mm("D", 0, TA, ones_k1[0:1, :], rows[0:1, 0:TA], True, True)
            bc = mm("D", 512, 512 + TA, ones_k1[0:1, :], rows[0:1, 512:512 + TA],
                    True, True, inc=True)
            ln_state = {"v": 0, "prev": {}, "bc": bc, "xc": xc}

            def ln_chunk(k):
                xck = xt[:, k * 1024 + par:k * 1024 + par + TA]
                t0 = pv(5120 + (k % 2) * 512, TA)
                if k == 0:
                    W.wait("vector", "mm", ln_state["bc"])
                    W.wait("vector", "x", ln_state["xc"])
                W.opn("vector", lambda e, t0=t0, x=xck: e.tensor_tensor(
                    t0, x, psD[:, 0:TA], ALU.subtract),
                    dep=(None if k < 2 else [ln_state["prev"][k - 2]]))
                v_ln = W.opn("vector", lambda e, t0=t0, k=k: e.tensor_tensor(
                    xnt[:, k * 1024:k * 1024 + TA], t0, psD[:, 512:512 + TA],
                    ALU.mult), dep=W.cnt["v"])
                ln_state["prev"][k] = v_ln
                ln_state["v"] = v_ln
                if k == KD - 1:
                    mark_read("D", 0, 1024, "v", v_ln)
            return ln_state, [lambda k=k: ln_chunk(k) for k in range(KD)]

        def ffn_tile(jt):
            W.vbar()
            W.barrier("scalar", ("mm", "v"))
            W.barrier("tensor", ("sc", "v"))
            W.barrier("vector", ("mm", "sc"))
            pe_guard("A", 0, 1024)
            pe_guard("B", 0, 1024)
            last_sq = {}
            for k in range(KD):
                xck = xt[:, k * 1024:(k + 1) * 1024]
                xq = pv(4096 + (k % 2) * 1024, TF)
                if k >= 2:
                    W.wait("scalar", "mm", last_sq[k - 2])
                csq = W.op("scalar", "sc", lambda e, o=xq, i=xck: e.activation(
                    o, i, AF.Square))
                W.wait("tensor", "sc", csq)
                for hh in range(2):
                    mm("A", hh * 512, (hh + 1) * 512, ones_col[:, 0:1],
                       xck[:, hh * 512:(hh + 1) * 512], k == 0, k == KD - 1,
                       p0=0, p1=1)
                    last_sq[k] = mm("B", hh * 512, (hh + 1) * 512, ones_col[:, 0:1],
                                    xq[:, hh * 512:(hh + 1) * 512], k == 0,
                                    k == KD - 1, inc=(hh == 1), p0=0, p1=1)
            stats_done = W.cnt["mm"]
            mean_r = rows[0:1, 0:TF]
            rstd_r = rows[0:1, TF:2 * TF]
            W.wait("vector", "mm", stats_done)
            v_mean = W.op("vector", "v", lambda e: e.tensor_scalar(
                mean_r, psA[0:1, 0:TF], 1.0 / C, None, ALU.mult))
            W.wait("scalar", "v", v_mean)
            c_m2 = W.op("scalar", "sc", lambda e: e.activation(
                pool[0:1, 0:TF], mean_r, AF.Square))
            W.wait("vector", "sc", c_m2)
            v_var = W.op("vector", "v", lambda e: e.scalar_tensor_tensor(
                pool[0:1, 0:TF], psB[0:1, 0:TF], 1.0 / C, pool[0:1, 0:TF],
                ALU.mult, ALU.subtract))
            mark_read("A", 0, 1024, "v", v_var)
            mark_read("B", 0, 1024, "v", v_var)
            W.wait("scalar", "v", v_var)
            c_sd = W.op("scalar", "sc", lambda e: e.activation(
                pool[0:1, 0:TF], pool[0:1, 0:TF], AF.Sqrt, bias=1e-5))
            W.wait("vector", "sc", c_sd)
            v_rst = W.op("vector", "v", lambda e: e.reciprocal(
                rstd_r, pool[0:1, 0:TF]))
            pe_guard("C", 0, 1024)
            pe_guard("D", 0, 1024)
            W.wait("tensor", "v", v_rst)
            for hh in range(2):
                mm("C", hh * 512, (hh + 1) * 512, ones_k1[0:1, :],
                   mean_r[0:1, hh * 512:(hh + 1) * 512], True, True)
                bc = mm("D", hh * 512, (hh + 1) * 512, ones_k1[0:1, :],
                        rstd_r[0:1, hh * 512:(hh + 1) * 512], True, True,
                        inc=(hh == 1))
            W.wait("vector", "mm", bc)
            pe_guard("A", 0, 1024)
            rt_mm = {}
            v_xn = 0
            for k in range(KD):
                xnf = pv((k % 2) * 1024, TF)
                xck = xt[:, k * 1024:(k + 1) * 1024]
                if k >= 2:
                    W.wait("vector", "mm", rt_mm[k - 2])
                W.opn("vector", lambda e, xnf=xnf, x=xck: e.tensor_tensor(
                    xnf, x, psC[:, 0:TF], ALU.subtract))
                W.opn("vector", lambda e, xnf=xnf: e.tensor_tensor(
                    xnf, xnf, psD[:, 0:TF], ALU.mult))
                v_xn = W.op("vector", "v", lambda e, xnf=xnf, k=k: e.tensor_copy(
                    xnt[:, k * 1024:(k + 1) * 1024], xnf))
                W.wait("tensor", "v", v_xn)
                for hh in range(2):
                    rt_mm[k] = mm("A", hh * 512, (hh + 1) * 512,
                                  gs_sb[:, k * 9:(k + 1) * 9],
                                  xnf[:, hh * 512:(hh + 1) * 512],
                                  k == 0, k == KD - 1, inc=(hh == 1),
                                  p0=0, p1=9)
            mark_read("C", 0, 1024, "v", v_xn)
            mark_read("D", 0, 1024, "v", v_xn)
            rt_sb = pool[0:9, 4096:4096 + TF]
            W.wait("vector", "mm", rt_mm[KD - 1])
            v_rt = W.op("vector", "v", lambda e: e.tensor_copy(
                rt_sb, psA[0:9, 0:TF]))
            mark_read("A", 0, 1024, "v", v_rt)
            pe_guard("C", 0, 512)
            W.wait("tensor", "v", v_rt)
            tp_c = 0
            for g in range(8):
                W.cnt["mm"] += 1
                tp_c = W.cnt["mm"]
                W.emit("tensor", lambda e, g=g: e.transpose(
                    psC[:, g * 9:(g + 1) * 9], rt_sb[0:9, g * 128:(g + 1) * 128],
                    ident[0:9, 0:9]).then_inc(W.handles["mm"], 1))
            lg = pool[:, 5120:5192]
            cmb = pool[:, 5248:5320]
            W.wait("vector", "mm", tp_c)
            v_lg = W.op("vector", "v", lambda e: e.tensor_copy(lg, psC[:, 0:72]))
            mark_read("C", 0, 512, "v", v_lg)
            sm = pool[:, 5376:5377]
            nm = pool[:, 5377:5378]
            ex8 = pool[:, 5384:5392]
            pr8 = pool[:, 5392:5400]
            eq8 = pool[:, 5400:5408]
            m1c = pool[:, 5408:5409]
            m2c = pool[:, 5409:5410]
            v_cmb = 0
            for g in range(8):
                v_nm = W.op("vector", "v", lambda e, g=g: e.reduce_max(
                    nm, lg[:, g * 9:g * 9 + 8], axis=AX.X, negate=True))
                W.wait("scalar", "v", v_nm)
                s_ex = W.op("scalar", "sc", lambda e, g=g: e.activation(
                    ex8, lg[:, g * 9:g * 9 + 8], AF.Exp, bias=nm[:, 0:1]))
                s_gs = W.op("scalar", "sc", lambda e, g=g: e.activation(
                    nm, lg[:, g * 9 + 8:g * 9 + 9], AF.Tanh, scale=0.5))
                W.wait("vector", "sc", s_gs)
                W.opn("vector", lambda e: e.reduce_sum(sm, ex8, axis=AX.X))
                W.opn("vector", lambda e: e.reciprocal(sm, sm))
                W.opn("vector", lambda e: e.tensor_scalar(
                    pr8, ex8, sm[:, 0:1], None, ALU.mult))
                W.opn("vector", lambda e: e.reduce_max(m1c, pr8, axis=AX.X))
                W.opn("vector", lambda e: e.tensor_scalar(
                    eq8, pr8, m1c[:, 0:1], None, ALU.is_equal))
                W.opn("vector", lambda e: e.scalar_tensor_tensor(
                    eq8, eq8, -2.0, pr8, ALU.mult, ALU.add))
                W.opn("vector", lambda e: e.reduce_max(m2c, eq8, axis=AX.X))
                W.opn("vector", lambda e: e.tensor_scalar(
                    eq8, pr8, m2c[:, 0:1], None, ALU.is_ge))
                W.opn("vector", lambda e, g=g: e.tensor_tensor(
                    cmb[:, g * 9:g * 9 + 8], pr8, eq8, ALU.mult))
                v_cmb = W.op("vector", "v", lambda e, g=g: e.tensor_scalar(
                    cmb[:, g * 9 + 8:g * 9 + 9], nm, 0.5, 0.5, ALU.mult, ALU.add))
            pe_guard("B", 0, 512)
            W.wait("tensor", "v", v_cmb)
            W.cnt["mm"] += 1
            c_tp = W.cnt["mm"]
            W.emit("tensor", lambda e: e.transpose(
                psB[0:72, 0:128], cmb, ident[:, :]).then_inc(W.handles["mm"], 1))
            W.wait("vector", "mm", c_tp)
            cT = pool[0:72, 4096:4096 + 128]
            v_cT = W.op("vector", "v", lambda e: e.tensor_copy(cT, psB[0:72, 0:128]))
            mark_read("B", 0, 512, "v", v_cT)
            W.wait("scalar", "v", v_cT)
            c_cmb = W.dma("scalar", "st", comb_d, cT)

            crow = rows[0:1, 0:TF]
            comb_r = comb_d.rearrange("(g i) p -> i g p", i=9)
            h_v = {}
            acc_v = {}
            prev_cb_mm = 0
            for pr_i in range(6):
                W.wait("vector", "mm", W.cnt["mm"])
                if pr_i == 0:
                    W.vbar()
                for ei in range(2):
                    e_idx = pr_i * 2 + ei
                    row_i = e_idx if e_idx < 8 else 8
                    hbuf = bigA if ei == 0 else bigB
                    W.wait("sync", "st", c_cmb)
                    W.wait("sync", "mm", prev_cb_mm)
                    c_row = W.dma("sync", "x", crow, comb_r[row_i])
                    pe_guard("C", 0, 1024)
                    W.wait("tensor", "x", c_row)
                    mm("C", 0, 512, ones_k1[0:1, :], crow[0:1, 0:512], True, True)
                    cb_mm = mm("C", 512, 1024, ones_k1[0:1, :],
                               crow[0:1, 512:1024], True, True, inc=True)
                    prev_cb_mm = cb_mm
                    W.wait("vector", "mm", cb_mm)
                    cb_v = W.op("vector", "v", lambda e: e.tensor_copy(
                        pv(5120, TF), psC[:, 0:TF]))
                    mark_read("C", 0, 1024, "v", cb_v)
                    for o in range(KF):
                        b1, w1 = load_strip(fc1_t[e_idx, o], C)
                        b2, w2 = load_strip(fc2_t[e_idx, o], C)
                        W.wait("tensor", "w", w2)
                        W.wait("tensor", "v", v_xn)
                        t1 = "A" if o % 2 == 0 else "C"
                        t2 = "B" if o % 2 == 0 else "D"
                        pe_guard(t1, 0, 1024)
                        for k in range(KD):
                            for hh in range(2):
                                a_mm = mm(t1, hh * 512, (hh + 1) * 512,
                                          wv(b1, k),
                                          xnt[:, k * 1024 + hh * 512:
                                              k * 1024 + (hh + 1) * 512],
                                          k == 0, k == KD - 1,
                                          inc=(hh == 1 and k == KD - 1))
                        pe_guard(t2, 0, 1024)
                        for k in range(KD):
                            for hh in range(2):
                                b_mm = mm(t2, hh * 512, (hh + 1) * 512,
                                          wv(b2, k),
                                          xnt[:, k * 1024 + hh * 512:
                                              k * 1024 + (hh + 1) * 512],
                                          k == 0, k == KD - 1,
                                          inc=(hh == 1 and k == KD - 1))
                        strips_done(2)
                        hsl = 2048 + (o % 2) * 1024
                        W.wait("scalar", "mm", a_mm)
                        if (e_idx, o - 2) in h_v:
                            W.wait("scalar", "v", h_v[(e_idx, o - 2)])
                        s_si = W.op("scalar", "sc", lambda e, hsl=hsl, t1=t1:
                                    e.activation(pool[:, hsl:hsl + TF],
                                                 PST[t1][:, 0:TF], AF.Silu))
                        mark_read(t1, 0, 1024, "sc", s_si)
                        W.wait("vector", "sc", s_si)
                        W.wait("vector", "mm", b_mm)
                        dep0 = [h_v.get((e_idx, o - 2)), cb_v]
                        W.opn("vector", lambda e, hsl=hsl, t2=t2: e.tensor_tensor(
                            pool[:, hsl:hsl + TF], pool[:, hsl:hsl + TF],
                            PST[t2][:, 0:TF], ALU.mult), dep=dep0)
                        hv = W.opn("vector", lambda e, hsl=hsl, hb=hbuf, o=o:
                                   e.tensor_tensor(hb[:, o * 1024:(o + 1) * 1024],
                                                   pool[:, hsl:hsl + TF],
                                                   pv(5120, TF), ALU.mult),
                                   dep=W.cnt["v"])
                        mark_read(t2, 0, 1024, "v", hv)
                        h_v[(e_idx, o)] = hv
                v_h_all = W.cnt["v"]
                for o2 in range(KD):
                    bufs = []
                    for ei in range(2):
                        bufs.append(load_strip(prj_t[pr_i * 2 + ei, o2], FM))
                    W.wait("tensor", "w", bufs[1][1])
                    W.wait("tensor", "v", v_h_all)
                    tn = "C" if o2 % 2 == 0 else "D"
                    pe_guard(tn, 0, 1024)
                    p_mm = None
                    for ei in range(2):
                        hb = bigA if ei == 0 else bigB
                        bpr = bufs[ei][0]
                        for k in range(KF):
                            for hh in range(2):
                                p_mm = mm(tn, hh * 512, (hh + 1) * 512,
                                          wv(bpr, k),
                                          hb[:, k * 1024 + hh * 512:
                                             k * 1024 + (hh + 1) * 512],
                                          ei == 0 and k == 0,
                                          ei == 1 and k == KF - 1,
                                          inc=(ei == 1 and k == KF - 1 and hh == 1))
                    strips_done(2)
                    W.wait("vector", "mm", p_mm)
                    v_acc = W.opn("vector", lambda e, tn=tn, o2=o2:
                                  e.tensor_tensor(
                                      xt[:, o2 * 1024:(o2 + 1) * 1024],
                                      xt[:, o2 * 1024:(o2 + 1) * 1024],
                                      PST[tn][:, 0:TF], ALU.add),
                                  dep=[acc_v.get(o2 - 2), acc_v.get(o2)])
                    mark_read(tn, 0, 1024, "v", v_acc)
                    acc_v[o2] = v_acc
            for k in range(KD):
                W.wait("scalar", "v", acc_v[k])
                W.dma("scalar", "st", yout[128 * k:128 * (k + 1), jt * TF:(jt + 1) * TF],
                      xt[:, k * 1024:(k + 1) * 1024])

        if debug:
            W.barrier("scalar", ("mm", "sc", "v"))
            W.dma("scalar", "st", dbg_stash, stash[:, 0:3072])
            W.dma("scalar", "st", dbg_c[:, 0:KH], c_c1[:, :])
            W.dma("scalar", "st", dbg_c[:, KH:2 * KH], c_c2[:, :])
            W.dma("scalar", "st", dbg_pool, pool[:, :])

        W.vbar()
        ln_state, ln_q = a2_ln_parts(0)
        for f in ln_q:
            f()
        ln_q = []

        for t in range(NTA):
            par = (t % 2) * 512
            v_ln = ln_state["v"]

            WLPS = (("A", 0), ("A", 512), ("B", 0), ("B", 512))

            def wl_stageA(o):
                buf, wc = load_strip(wl_t[o], C)
                W.wait("tensor", "w", wc)
                W.wait("tensor", "v", v_ln)
                tn, lo = WLPS[o % 4]
                pe_guard(tn, lo, lo + TA)
                g_mm = None
                for k in range(KD):
                    g_mm = mm(tn, lo, lo + TA, wv(buf, k),
                              xnt[:, k * 1024:k * 1024 + TA],
                              k == 0, k == KD - 1, inc=(k == KD - 1))
                strips_done(1)
                uh = (o % 2) * 512
                W.wait("scalar", "mm", g_mm)
                lr_m2 = lr_v.get((t, o - 2)) if o >= 2 else (
                    lr_v.get((t - 1, KH - 2 + o)) if t > 0 else None)
                if lr_m2:
                    W.wait("scalar", "v", lr_m2)
                W.opn("scalar", lambda e, uh=uh, tn=tn, lo=lo, o=o: e.activation(
                    pool[:, uh:uh + TA], PST[tn][:, lo:lo + TA], AF.Identity,
                    bias=c_bl[:, o:o + 1]))
                s_u = W.op("scalar", "sc", lambda e, uh=uh: e.activation(
                    pool[:, 1024 + uh:1024 + uh + TA], pool[:, uh:uh + TA],
                    AF.Square))
                mark_read(tn, lo, lo + 512, "sc", s_u)
                return s_u, lr_m2

            state = {}
            for o in range(KH + 1):
                if o < KH:
                    rsl = (o % 2) * 512
                    key = (t, o - 2) if o >= 2 else (t - 1, KH - 2 + o)
                    if key in rlld_v:
                        W.wait("sync", "v", rlld_v[key])
                    W.dma("sync", "x", bigA[:, rsl:rsl + TA],
                          rl_sc[o * 128:(o + 1) * 128, t * TA:(t + 1) * TA])
                    rlld_x[(t, o)] = W.dma(
                        "sync", "x", bigA[:, 1024 + rsl:1024 + rsl + TA],
                        ar_sc[o * 128:(o + 1) * 128, t * TA:(t + 1) * TA])
                    s_u, lr_m2 = wl_stageA(o)
                    uh = (o % 2) * 512
                    W.wait("vector", "sc", s_u)
                    v1 = W.opn("vector", lambda e, uh=uh: e.tensor_tensor(
                        pool[:, 2048 + uh:2048 + uh + TA], pool[:, uh:uh + TA],
                        pool[:, 1024 + uh:1024 + uh + TA], ALU.mult),
                        dep=[lr_m2])
                else:
                    v1 = None
                if (o - 1) >= 0:
                    po = o - 1
                    ph = (po % 2) * 512
                    st_b = state[po]
                    W.wait("vector", "sc", st_b["th"])
                    st_b["zz"] = W.opn("vector", lambda e, ph=ph: e.tensor_scalar(
                        pool[:, 4096 + ph:4096 + ph + TA],
                        pool[:, 4096 + ph:4096 + ph + TA],
                        0.5, 0.5, ALU.mult, ALU.add),
                        dep=[lr_v.get((t, po - 2)) if po >= 2 else
                             (lr_v.get((t - 1, KH - 2 + po)) if t > 0 else None),
                             st_b["vw"]])
                if o < KH:
                    vw = W.opn("vector", lambda e, uh=uh: e.scalar_tensor_tensor(
                        pool[:, 3072 + uh:3072 + uh + TA],
                        pool[:, 2048 + uh:2048 + uh + TA], 0.044715,
                        pool[:, uh:uh + TA], ALU.mult, ALU.add), dep=[v1])
                    W.wait("scalar", "v", vw)
                    s_th = W.op("scalar", "sc", lambda e, uh=uh: e.activation(
                        pool[:, 4096 + uh:4096 + uh + TA],
                        pool[:, 3072 + uh:3072 + uh + TA], AF.Tanh, scale=GC))
                    state[o] = {"vw": vw, "th": s_th, "v1": v1}
                if (o - 1) >= 0:
                    po = o - 1
                    ph = (po % 2) * 512
                    st_b = state[po]
                    W.opn("vector", lambda e, ph=ph: e.tensor_tensor(
                        pool[:, ph:ph + TA], pool[:, ph:ph + TA],
                        pool[:, 4096 + ph:4096 + ph + TA], ALU.mult),
                        dep=[st_b["zz"], st_b["vw"]])
                    W.wait("vector", "x", rlld_x[(t, po)])
                    if t == 0 and po == 0:
                        W.wait("vector", "x", hsel_x)
                    if t == 0 and po < 4:
                        W.wait("vector", "st", c_carry)
                    prl = (po % 2) * 512
                    W.opn("vector", lambda e, prl=prl, po=po, ph=ph:
                          e.scalar_tensor_tensor(
                              pool[:, 1024 + ph:1024 + ph + TA],
                              bigA[:, 1024 + prl:1024 + prl + TA],
                              hsel[:, po:po + 1], bigA[:, prl:prl + TA],
                              ALU.mult, ALU.add), dep=[st_b["v1"]])
                    lr = W.opn("vector", lambda e, po=po, ph=ph: e.tensor_tensor(
                        bigB[:, po * 512:po * 512 + TA], pool[:, ph:ph + TA],
                        pool[:, 1024 + ph:1024 + ph + TA], ALU.mult),
                        dep=W.cnt["v"])
                    rlld_v[(t, po)] = lr
                    lr_v[(t, po)] = lr
            v_lr_all = W.cnt["v"]

            if t % 2 == 0:
                ln_state, ln_q = a2_ln_parts(t + 1)
            else:
                ln_q = []

            for o2 in range(KD):
                buf, wc = load_strip(wo_t[o2], HID)
                W.wait("tensor", "w", wc)
                W.wait("tensor", "v", v_lr_all)
                lo = (o2 % 2) * 512
                pe_guard("B", lo, lo + TA)
                for k in range(KH):
                    g_mm = mm("B", lo, lo + TA, wv(buf, k),
                              bigB[:, k * 512:k * 512 + TA],
                              k == 0, k == KH - 1, inc=(k == KH - 1))
                strips_done(1)
                W.wait("vector", "mm", g_mm)
                v_r = W.opn("vector", lambda e, lo=lo, o2=o2, par=par:
                            e.scalar_tensor_tensor(
                                xt[:, o2 * 1024 + par:o2 * 1024 + par + TA],
                                psB[:, lo:lo + TA],
                                c_bo[:, o2:o2 + 1],
                                xt[:, o2 * 1024 + par:o2 * 1024 + par + TA],
                                ALU.add, ALU.add))
                mark_read("B", lo, lo + 512, "v", v_r)
                for _ in range(2):
                    if ln_q:
                        ln_q.pop(0)()
            v_a2[t] = W.cnt["v"]
            if t == 1:
                ffn_tile(0)
                v_a2[1] = W.cnt["v"]
                W.vbar()
                W.wait("vector", "mm", W.cnt["mm"])
                W.wait("sync", "st", W.cnt["st"])
                ln_state, ln_q = a2_ln_parts(2)
                for f in ln_q:
                    f()
                ln_q = []
            if t == 3:
                ffn_tile(1)
        W.wait("scalar", "st", W.cnt["st"])
        W.wait("sync", "st", W.cnt["st"])

        with ExitStack() as es2:
            for s in W.cnt:
                W.handles[s] = es2.enter_context(nc.semaphore(s))
            block = es2.enter_context(nc.Block())
            W.run(block)

    return nc


def _prep_inputs(kw):
    f32 = np.float32
    x = kw["L0_attn_x"]
    xT = np.ascontiguousarray(np.asarray(x, dtype=f32).transpose(0, 2, 1))

    def tile_w(w, kchunks, ochunks):
        r = np.asarray(w, dtype=f32).reshape(kchunks, 128, ochunks, 128)
        return np.ascontiguousarray(
            r.transpose(2, 1, 0, 3).reshape(ochunks, 128, kchunks * 128))

    wr = tile_w(kw["L0_attn_Wr"], KD, KH)
    wl = tile_w(kw["L0_attn_Wl"], KD, KH)
    wa = tile_w(kw["L0_attn_Wa"], KH, KH)
    wi = tile_w(kw["L0_attn_Wi"], KH, KH)
    wo = tile_w(kw["L0_attn_Wo"], KH, KD)

    fc1 = np.concatenate([np.asarray(kw["L0_ffn_expert_fc1"], f32),
                          np.asarray(kw["L0_ffn_shared_fc1"], f32).reshape(4, FM, C)],
                         axis=0)
    fc2 = np.concatenate([np.asarray(kw["L0_ffn_expert_fc2"], f32),
                          np.asarray(kw["L0_ffn_shared_fc2"], f32).reshape(4, FM, C)],
                         axis=0)
    prj = np.concatenate([np.asarray(kw["L0_ffn_expert_proj"], f32),
                          np.asarray(kw["L0_ffn_shared_proj"], f32)
                          .reshape(C, 4, FM).transpose(1, 0, 2)], axis=0)
    fc1_t = np.stack([tile_w(fc1[e].T, KD, KF) for e in range(NE)])
    fc2_t = np.stack([tile_w(fc2[e].T, KD, KF) for e in range(NE)])
    prj_t = np.stack([tile_w(prj[e].T, KF, KD) for e in range(NE)])

    gct = np.concatenate([np.asarray(kw["L0_ffn_gate_w"], f32),
                          np.asarray(kw["L0_ffn_shared_gate_w"], f32)], axis=0)
    gs = np.ascontiguousarray(
        gct.T.reshape(KD, 128, 9).transpose(1, 0, 2).reshape(128, KD * 9))

    def vecT(v, kchunks):
        return np.ascontiguousarray(np.asarray(v, f32).reshape(kchunks, 128).T)

    consts = dict(
        blT=vecT(kw["L0_attn_bl"], KH), brT=vecT(kw["L0_attn_br"], KH),
        cbT=vecT(kw["L0_attn_conv_b"], KH), baT=vecT(kw["L0_attn_ba"], KH),
        biT=vecT(kw["L0_attn_bi"], KH), lamT=vecT(kw["L0_attn_lam"], KH),
        boT=vecT(kw["L0_attn_bo"], KD),
        cwT=np.ascontiguousarray(np.concatenate(
            [vecT(np.asarray(kw["L0_attn_conv_w"], f32)[t, 0], KH)
             for t in range(4)], axis=1)),
        identT=np.eye(128, dtype=f32),
        gs_t=gs,
        fconst=np.tile(np.array([[-1.0, 1.0, 1e-5, 0.5, GC, 0.0, 2.0, -2.0]], f32), (128, 1)),
    )

    in_maps = []
    for c in range(8):
        b, h = c // 2, c % 2
        m = dict(xin=np.ascontiguousarray(xT[b, :, h * TC:(h + 1) * TC]),
                 wr_t=wr, wl_t=wl, wa_t=wa, wi_t=wi, wo_t=wo,
                 fc1_t=fc1_t, fc2_t=fc2_t, prj_t=prj_t)
        if h == 1:
            xpre = xT[b, :, h * TC - 3:h * TC]
        else:
            xpre = np.zeros((C, 3), f32)
        m["xpreT"] = np.ascontiguousarray(xpre.reshape(KD, 128, 3))
        m["maskT"] = np.full((128, 1), float(h), f32)
        sel = np.zeros((8, 1), f32)
        if h == 1:
            sel[c - 1, 0] = 1.0
        m["selT"] = sel
        m.update(consts)
        in_maps.append(m)
    return in_maps


def kernel(**kw):
    in_maps = _prep_inputs(kw)
    nc = build_program()
    res = run_bass_kernel_spmd(nc, in_maps, core_ids=list(range(8)))
    out = np.empty((4, 4096, 2048), np.float32)
    for c in range(8):
        b, h = c // 2, c % 2
        out[b, h * TC:(h + 1) * TC, :] = res.results[c]["yout"].T
    return out


# version 08 (final)
# speedup vs baseline: 1.0576x; 1.0004x over previous
import numpy as np
import concourse.bass as bass
import concourse.mybir as mybir
from concourse.bass_utils import run_bass_kernel_spmd
from contextlib import ExitStack

F32 = mybir.dt.float32
BF16 = mybir.dt.bfloat16
AF = mybir.ActivationFunctionType
ALU = mybir.AluOpType
AX = mybir.AxisListType

C = 2048
HID = 2560
KD = 16
KH = 20
KF = 11
FM = 1408
NE = 12
TC = 2048
TA = 512
NTA = 4
TF = 1024
NTF = 2
GC = 0.7978845608028654
ENGINES = ("sync", "gpsimd", "tensor", "vector", "scalar")
RG8 = [[0, 1, 2, 3, 4, 5, 6, 7]]


class B:

    def __init__(self):
        self.streams = {e: [] for e in ENGINES}
        self.cnt = {s: 0 for s in ("w", "x", "st", "mm", "sc", "v", "cc")}
        self.handles = {}
        self.last_wait = {e: {} for e in ENGINES}

    def emit(self, eng, fn):
        self.streams[eng].append(fn)

    def wait(self, eng, s, val):
        if val is None or val <= 0:
            return
        lw = self.last_wait[eng]
        if lw.get(s, -1) >= val:
            return
        lw[s] = val
        h = self.handles
        self.emit(eng, lambda e, s=s, v=val: e.wait_ge(h[s], v))

    def vbar(self):
        self.wait("vector", "v", self.cnt["v"])

    def barrier(self, eng, sems=("mm", "sc", "v")):
        for s in sems:
            self.wait(eng, s, self.cnt[s])

    def dma(self, eng, sname, out, in_):
        self.cnt[sname] += 16
        c = self.cnt[sname]
        h = self.handles
        self.emit(eng, lambda e, o=out, i=in_, s=sname: e.dma_start(out=o, in_=i)
                  .then_inc(h[s], 16))
        return c

    def op(self, eng, sname, fn, dep=None):
        if eng == "vector":
            assert sname == "v"
            return self.opn(eng, fn, dep=dep)
        self.cnt[sname] += 1
        c = self.cnt[sname]
        h = self.handles
        self.emit(eng, lambda e, fn=fn, s=sname: fn(e).then_inc(h[s], 1))
        return c

    def opn(self, eng, fn, dep=None):
        if eng == "vector":
            if dep is None:
                dep = self.cnt["v"]
            elif isinstance(dep, (list, tuple)):
                dep = max([d for d in dep if d] or [0])
            self.wait(eng, "v", dep)
            self.cnt["v"] += 1
            c = self.cnt["v"]
            h = self.handles
            self.emit(eng, lambda e, fn=fn: fn(e).then_inc(h["v"], 1))
            return c
        self.emit(eng, lambda e, fn=fn: fn(e))

    def run(self, block):
        for eng_name in ENGINES:
            ops = self.streams[eng_name]
            getattr(block, eng_name)(lambda e, ops=ops: [f(e) for f in ops])


def build_program(debug=False):
    nc = bass.Bass("TRN2", target_bir_lowering=False)
    dbg_kind = "ExternalOutput" if debug else "Internal"

    def din(name, shape):
        return nc.dram_tensor(name, list(shape), F32, kind="ExternalInput").ap()

    xin = din("xin", (C, TC))
    xpreT = din("xpreT", (KD, 128, 3))
    maskT = din("maskT", (128, 1))
    selT = din("selT", (8, 1))
    identT = din("identT", (128, 128))
    wr_t = din("wr_t", (KH, 128, C))
    wl_t = din("wl_t", (KH, 128, C))
    wa_t = din("wa_t", (KH, 128, HID))
    wi_t = din("wi_t", (KH, 128, HID))
    wo_t = din("wo_t", (KD, 128, HID))
    fc1_t = din("fc1_t", (NE, KF, 128, C))
    fc2_t = din("fc2_t", (NE, KF, 128, C))
    prj_t = din("prj_t", (NE, KD, 128, FM))
    gs_t = din("gs_t", (128, KD * 9))
    blT = din("blT", (128, KH))
    brT = din("brT", (128, KH))
    cbT = din("cbT", (128, KH))
    baT = din("baT", (128, KH))
    biT = din("biT", (128, KH))
    lamT = din("lamT", (128, KH))
    boT = din("boT", (128, KD))
    cwT = din("cwT", (128, 4 * KH))
    fconst = din("fconst", (128, 8))

    yout = nc.dram_tensor("yout", [C, TC], F32, kind="ExternalOutput").ap()

    xmid = nc.dram_tensor("xmid", [C, TC], F32, kind=dbg_kind).ap()
    rl_sc = nc.dram_tensor("rl_sc", [HID, TC], BF16, kind=dbg_kind).ap()
    ar_sc = nc.dram_tensor("ar_sc", [HID, TC], BF16, kind=dbg_kind).ap()
    rows_d = nc.dram_tensor("rows_d", [NTA, 1024], F32, kind=dbg_kind).ap()
    cc_in = nc.dram_tensor("cc_in", [1, HID], F32).ap()
    cc_out = nc.dram_tensor("cc_out", [8, HID], F32).ap()
    hsel_d = nc.dram_tensor("hsel_d", [1, HID], F32, kind=dbg_kind).ap()
    comb_d = nc.dram_tensor("comb_d", [72, 128], F32, kind=dbg_kind).ap()
    if debug:
        dbg_stash = nc.dram_tensor("dbg_stash", [128, 3072], F32, kind="ExternalOutput").ap()
        dbg_c = nc.dram_tensor("dbg_c", [128, 2 * KH], F32, kind="ExternalOutput").ap()
        dbg_pool = nc.dram_tensor("dbg_pool", [128, 6144], F32, kind="ExternalOutput").ap()

    es = ExitStack()

    def sbuf(name, shape, dtype=F32):
        return es.enter_context(nc.sbuf_tensor(name, list(shape), dtype))

    def psum(name, shape):
        return es.enter_context(nc.psum_tensor(name, list(shape), F32))

    with es:
        xt = sbuf("xt", (128, KD * 1024))
        xnt = sbuf("xnt", (128, KD * 1024), BF16)
        bigA = sbuf("bigA", (128, KF * 1024), BF16)
        bigB = sbuf("bigB", (128, KF * 1024), BF16)
        stash = sbuf("stash", (128, 4 * 512))
        stashz = sbuf("stashz", (128, 2 * 512), BF16)
        xsqb = sbuf("xsqb", (128, 2 * 512), BF16)
        wbuf = sbuf("wbuf", (128, 4 * HID), BF16)
        pool = sbuf("pool", (128, 6144))
        rows = sbuf("rows", (1, 2048))
        gs_sb = sbuf("gs_sb", (128, KD * 9))
        ident = sbuf("ident", (128, 128))
        ones_k1 = sbuf("ones_k1", (1, 128))
        ones_col = sbuf("ones_col", (128, 1))
        ones_colb = sbuf("ones_colb", (128, 1), BF16)
        c_bl = sbuf("c_bl", (128, KH))
        c_br = sbuf("c_br", (128, KH))
        c_cb = sbuf("c_cb", (128, KH))
        c_hba = sbuf("c_hba", (128, KH))
        c_hbi = sbuf("c_hbi", (128, KH))
        c_lam = sbuf("c_lam", (128, KH))
        c_c1 = sbuf("c_c1", (128, KH))
        c_c2 = sbuf("c_c2", (128, KH))
        c_bo = sbuf("c_bo", (128, KD))
        c_cw = sbuf("c_cw", (128, 4 * KH))
        c_mask = sbuf("c_mask", (128, 1))
        sel_sb = sbuf("sel_sb", (8, 1))
        carry = sbuf("carry", (128, KH))
        prefp = sbuf("prefp", (128, KH))
        rtail = sbuf("rtail", (128, 3 * KH))
        hsel = sbuf("hsel", (128, KH))
        fc_sb = sbuf("fc_sb", (128, 8))
        FVALS = (-1.0, 1.0, 1e-5, 0.5, GC, 0.0, 2.0, -2.0)
        for i, fv in enumerate(FVALS):
            nc.const_aps.register(F32, fv, fc_sb[:, i:i + 1])

        psA = psum("psA", (128, 1024))
        psB = psum("psB", (128, 1024))
        psC = psum("psC", (128, 1024))
        psD = psum("psD", (128, 1024))
        PST = {"A": psA, "B": psB, "C": psC, "D": psD}

        W = B()

        ps_reader = {}

        def regions(tn, lo, hi):
            out = []
            if lo < 512:
                out.append((tn, 0))
            if hi > 512:
                out.append((tn, 1))
            return out

        def pe_guard(tn, lo, hi):
            for r in regions(tn, lo, hi):
                if r in ps_reader:
                    s, cv = ps_reader[r]
                    W.wait("tensor", s, cv)

        def mark_read(tn, lo, hi, sem, cntv):
            for r in regions(tn, lo, hi):
                ps_reader[r] = (sem, cntv)

        def mm(tn, lo, hi, lhsT, rhs, start, stop, inc=False, p0=0, p1=128):
            t = PST[tn]
            if inc:
                W.cnt["mm"] += 1
                c = W.cnt["mm"]
                h = W.handles
                W.emit("tensor", lambda e, t=t, lo=lo, hi=hi, l=lhsT, r=rhs,
                       s1=start, s2=stop, p0=p0, p1=p1: e.matmul(
                           t[p0:p1, lo:hi], l, r, start=s1,
                           stop=s2).then_inc(h["mm"], 1))
                return c
            W.emit("tensor", lambda e, t=t, lo=lo, hi=hi, l=lhsT, r=rhs,
                   s1=start, s2=stop, p0=p0, p1=p1: e.matmul(
                       t[p0:p1, lo:hi], l, r, start=s1, stop=s2))
            return None

        strip_state = {"idx": 0, "free": []}

        def load_strip(src_ap, cols):
            idx = strip_state["idx"]
            strip_state["idx"] += 1
            buf = idx % 4
            if idx >= 4:
                W.wait("gpsimd", "mm", strip_state["free"][idx - 4])
            dst = wbuf[:, buf * HID:buf * HID + cols]
            W.cnt["w"] += 16
            wc = W.cnt["w"]
            h = W.handles
            W.emit("gpsimd", lambda e, d=dst, s=src_ap: e.dma_start(out=d, in_=s)
                   .then_inc(h["w"], 16))
            return buf, wc

        def strips_done(n):
            for _ in range(n):
                strip_state["free"].append(W.cnt["mm"])

        def wv(buf, k, n=128):
            return wbuf[:, buf * HID + k * 128:buf * HID + k * 128 + n]

        def pv(lo, n, p0=0, p1=128):
            return pool[p0:p1, lo:lo + n]

        for src, dst in ((blT, c_bl), (brT, c_br), (cbT, c_cb), (baT, c_hba),
                         (biT, c_hbi), (lamT, c_lam), (boT, c_bo), (cwT, c_cw),
                         (maskT, c_mask), (selT, sel_sb), (identT, ident),
                         (gs_t, gs_sb), (fconst, fc_sb)):
            W.dma("sync", "x", dst[:, :], src)
        for k in range(KD):
            W.dma("sync", "x", xt[:, k * 1024 + 512:k * 1024 + 515], xpreT[k])
        x_consts = W.cnt["x"]

        W.opn("vector", lambda e: e.memset(ones_k1[:, :], 1.0))
        W.opn("vector", lambda e: e.memset(ones_col[:, :], 1.0))
        W.opn("vector", lambda e: e.memset(ones_colb[:, :], 1.0))
        W.opn("vector", lambda e: e.memset(carry[:, :], 0.0))
        W.opn("vector", lambda e: e.memset(prefp[:, :], 1.0))
        W.wait("vector", "x", x_consts)
        W.op("vector", "v", lambda e: e.tensor_scalar(
            c_hba[:, :], c_hba[:, :], 0.5, None, ALU.mult))
        W.op("vector", "v", lambda e: e.tensor_scalar(
            c_hbi[:, :], c_hbi[:, :], 0.5, None, ALU.mult))
        W.wait("scalar", "x", x_consts)
        W.op("scalar", "sc", lambda e: e.activation(
            c_c1[:, :], c_lam[:, :], AF.Exp, scale=-1.0))
        sc_sp = W.op("scalar", "sc", lambda e: e.activation(
            c_c1[:, :], c_c1[:, :], AF.Ln, bias=1.0))
        W.wait("vector", "sc", sc_sp)
        W.op("vector", "v", lambda e: e.tensor_scalar(
            c_c1[:, :], c_c1[:, :], -4.0, None, ALU.mult))
        W.op("vector", "v", lambda e: e.tensor_scalar(
            c_c2[:, :], c_c1[:, :], 2.0, None, ALU.mult))

        def ln_block(t0col, ncols, save_tile=None):
            W.barrier("scalar", ("mm", "v"))
            W.barrier("tensor", ("sc", "v"))
            W.barrier("vector", ("mm", "sc"))
            pe_guard("D", 0, 1024)
            last_sq = {}
            for k in range(KD):
                xck = xt[:, k * 1024 + t0col:k * 1024 + t0col + ncols]
                xq = xsqb[:, (k % 2) * 512:(k % 2) * 512 + ncols]
                if k >= 2:
                    W.wait("scalar", "mm", last_sq[k - 2])
                csq = W.op("scalar", "sc", lambda e, o=xq, i=xck: e.activation(
                    o, i, AF.Square))
                W.wait("tensor", "sc", csq)
                mm("D", 0, ncols, ones_col[:, 0:1], xck, k == 0, k == KD - 1,
                   p0=0, p1=1)
                last_sq[k] = mm("D", 512, 512 + ncols, ones_colb[:, 0:1], xq,
                                k == 0, k == KD - 1, inc=True, p0=0, p1=1)
            stats_done = W.cnt["mm"]
            mean_r = rows[0:1, 0:ncols]
            rstd_r = rows[0:1, 512:512 + ncols]
            m2_r = rows[0:1, 1024:1024 + ncols]
            var_r = rows[0:1, 1536:1536 + ncols]
            W.wait("vector", "mm", stats_done)
            v_mean = W.op("vector", "v", lambda e: e.tensor_scalar(
                mean_r, psD[0:1, 0:ncols], 1.0 / C, None, ALU.mult))
            W.wait("scalar", "v", v_mean)
            c_m2 = W.op("scalar", "sc", lambda e: e.activation(m2_r, mean_r, AF.Square))
            W.wait("vector", "sc", c_m2)
            v_var = W.op("vector", "v", lambda e: e.scalar_tensor_tensor(
                var_r, psD[0:1, 512:512 + ncols], 1.0 / C, m2_r,
                ALU.mult, ALU.subtract))
            W.wait("scalar", "v", v_var)
            c_sd = W.op("scalar", "sc", lambda e: e.activation(
                var_r, var_r, AF.Sqrt, bias=1e-5))
            W.wait("vector", "sc", c_sd)
            v_rst = W.op("vector", "v", lambda e: e.reciprocal(rstd_r, var_r))
            mark_read("D", 0, 1024, "v", v_var)
            if save_tile is not None:
                W.wait("scalar", "v", v_rst)
                W.dma("scalar", "st", rows_d[save_tile], rows[0:1, 0:1024])
            pe_guard("D", 0, 1024)
            W.wait("tensor", "v", v_rst)
            mm("D", 0, ncols, ones_k1[0:1, :], mean_r, True, True)
            bc = mm("D", 512, 512 + ncols, ones_k1[0:1, :], rstd_r, True, True,
                    inc=True)
            W.wait("vector", "mm", bc)
            v_last = 0
            lnp = {}
            for k in range(KD):
                xck = xt[:, k * 1024 + t0col:k * 1024 + t0col + ncols]
                t0 = pv(1030 + (k % 2) * 512, ncols)
                W.opn("vector", lambda e, t0=t0, x=xck: e.tensor_tensor(
                    t0, x, psD[:, 0:ncols], ALU.subtract),
                    dep=(None if k < 2 else [lnp[k - 2]]))
                v_last = W.opn("vector", lambda e, t0=t0, k=k: e.tensor_tensor(
                    xnt[:, k * 1024 + t0col:k * 1024 + t0col + ncols], t0,
                    psD[:, 512:512 + ncols], ALU.mult), dep=W.cnt["v"])
                lnp[k] = v_last
            mark_read("D", 0, 1024, "v", v_last)
            return v_last

        W.wait("scalar", "x", x_consts)
        W.wait("vector", "x", x_consts)
        W.wait("tensor", "x", x_consts)
        v_xnpre = ln_block(512, 3)

        spill_st = {}
        conv_v = {}
        mm_stats = [0] * NTA
        v_ln1 = [0] * NTA
        mm_wawi = [0] * NTA
        grp_v = {}

        def a1_load_ln(t):
            W.wait("sync", "v", v_ln1[t - 1] if t > 0 else 0)
            W.wait("sync", "mm", mm_stats[t - 1] if t > 0 else 0)
            xc = 0
            for k in range(KD):
                xc = W.dma("sync", "x", xt[:, k * 1024:k * 1024 + TA],
                           xin[128 * k:128 * (k + 1), t * TA:(t + 1) * TA])
            W.wait("scalar", "x", xc)
            W.wait("tensor", "x", xc)
            W.wait("vector", "x", xc)
            v_ln1[t] = ln_block(0, TA, save_tile=t)
            mm_stats[t] = W.cnt["mm"]

        a1_load_ln(0)
        for t in range(NTA):
            for o in range(KH):
                buf, wc = load_strip(wr_t[o], C)
                W.wait("tensor", "w", wc)
                W.wait("tensor", "v", v_ln1[t])
                lo = (o % 2) * 512
                pe_guard("A", lo, lo + TA)
                for k in range(KD):
                    g_mm = mm("A", lo, lo + TA, wv(buf, k),
                              xnt[:, k * 1024:k * 1024 + TA],
                              k == 0, k == KD - 1, inc=(k == KD - 1))
                p_mm = None
                if t == 0:
                    W.wait("tensor", "v", v_xnpre)
                    pe_guard("B", 512, 1024)
                    for k in range(KD):
                        p_mm = mm("B", 512, 515, wv(buf, k),
                                  xnt[:, k * 1024 + 512:k * 1024 + 515],
                                  k == 0, k == KD - 1, inc=(k == KD - 1))
                strips_done(1)
                rh = (o % 2) * 515
                W.wait("scalar", "mm", g_mm)
                if o >= 2:
                    W.wait("scalar", "v", conv_v[(t, o - 2)])
                e_sc = W.op("scalar", "sc", lambda e, rh=rh, lo=lo, o=o: e.activation(
                    pool[:, rh + 3:rh + 3 + TA], psA[:, lo:lo + TA], AF.Identity,
                    bias=c_br[:, o:o + 1]))
                mark_read("A", lo, lo + 512, "sc", e_sc)
                if t == 0:
                    W.wait("scalar", "mm", p_mm)
                    e_pre = W.op("scalar", "sc", lambda e, o=o: e.activation(
                        pool[:, 2566:2569], psB[:, 512:515], AF.Identity,
                        bias=c_br[:, o:o + 1]))
                    mark_read("B", 512, 1024, "sc", e_pre)
                    W.wait("vector", "sc", e_pre)
                    W.op("vector", "v", lambda e, o=o: e.tensor_scalar(
                        rtail[:, 3 * o:3 * o + 3], pool[:, 2566:2569],
                        c_mask[:, 0:1], None, ALU.mult))
                W.wait("vector", "sc", e_sc)
                W.opn("vector", lambda e, rh=rh, o=o: e.tensor_copy(
                    pool[:, rh:rh + 3], rtail[:, 3 * o:3 * o + 3]))
                W.opn("vector", lambda e, rh=rh, o=o: e.tensor_scalar(
                    pool[:, 1030:1030 + TA], pool[:, rh:rh + TA],
                    c_cw[:, o:o + 1], c_cb[:, o:o + 1], ALU.mult, ALU.add))
                for tp in range(1, 4):
                    W.opn("vector", lambda e, rh=rh, o=o, tp=tp: e.scalar_tensor_tensor(
                        pool[:, 1030:1030 + TA], pool[:, rh + tp:rh + tp + TA],
                        c_cw[:, tp * KH + o:tp * KH + o + 1],
                        pool[:, 1030:1030 + TA], ALU.mult, ALU.add))
                W.opn("vector", lambda e, rh=rh, o=o: e.tensor_copy(
                    rtail[:, 3 * o:3 * o + 3], pool[:, rh + TA:rh + TA + 3]))
                if t > 0:
                    W.wait("vector", "mm", mm_wawi[t - 1])
                conv_v[(t, o)] = W.op("vector", "v", lambda e, o=o: e.tensor_copy(
                    bigA[:, o * 512:(o + 1) * 512], pool[:, 1030:1030 + TA]))
            v_conv_all = W.cnt["v"]
            if t + 1 < NTA:
                W.vbar()
                a1_load_ln(t + 1)

            for grp in range(KH // 2):
                if grp == 0:
                    W.wait("scalar", "v", v_conv_all)
                else:
                    W.wait("scalar", "v", grp_v[(t, grp - 1)])
                mm_r = {}
                mm_i = {}
                for j in range(2):
                    o = grp * 2 + j
                    bufA, wA = load_strip(wa_t[o], HID)
                    bufI, wI = load_strip(wi_t[o], HID)
                    lo = (o % 2) * 512
                    W.wait("tensor", "w", wI)
                    W.wait("tensor", "v", v_conv_all)
                    pe_guard("B", lo, lo + TA)
                    for k in range(KH):
                        mm_r[j] = mm("B", lo, lo + TA, wv(bufA, k),
                                     bigA[:, k * 512:k * 512 + TA],
                                     k == 0, k == KH - 1, inc=(k == KH - 1))
                    pe_guard("C", lo, lo + TA)
                    for k in range(KH):
                        mm_i[j] = mm("C", lo, lo + TA, wv(bufI, k),
                                     bigA[:, k * 512:k * 512 + TA],
                                     k == 0, k == KH - 1, inc=(k == KH - 1))
                    strips_done(2)
                for j in range(2):
                    o = grp * 2 + j
                    lo = (o % 2) * 512
                    zsl = stashz[:, j * 512:(j + 1) * 512]
                    dsl = stash[:, (0 + j) * 512:(1 + j) * 512]
                    asl = stash[:, (2 + j) * 512:(3 + j) * 512]
                    W.wait("scalar", "mm", mm_r[j])
                    W.opn("scalar", lambda e, lo=lo, o=o: e.activation(
                        pool[:, 2566:2566 + TA], psB[:, lo:lo + TA], AF.Tanh,
                        scale=0.5, bias=c_hba[:, o:o + 1]))
                    W.opn("scalar", lambda e, o=o, asl=asl: e.activation(
                        asl, pool[:, 2566:2566 + TA], AF.Exp,
                        scale=c_c1[:, o:o + 1], bias=c_c1[:, o:o + 1]))
                    s_a2 = W.op("scalar", "sc", lambda e, o=o, zsl=zsl: e.activation(
                        zsl, pool[:, 2566:2566 + TA], AF.Exp,
                        scale=c_c2[:, o:o + 1], bias=c_c2[:, o:o + 1]))
                    mark_read("B", lo, lo + 512, "sc", s_a2)
                    W.wait("scalar", "mm", mm_i[j])
                    s_ti = W.op("scalar", "sc", lambda e, lo=lo, o=o, dsl=dsl:
                                e.activation(dsl, psC[:, lo:lo + TA], AF.Tanh,
                                             scale=0.5, bias=c_hbi[:, o:o + 1]))
                    mark_read("C", lo, lo + 512, "sc", s_ti)
                    W.wait("vector", "sc", s_ti)
                    W.opn("vector", lambda e, zsl=zsl: e.tensor_scalar(
                        zsl, zsl, -1.0, 1.0, ALU.mult, ALU.add))
                    W.opn("vector", lambda e, zsl=zsl: e.tensor_scalar(
                        zsl, zsl, 1e-12, None, ALU.max))
                    W.opn("vector", lambda e, dsl=dsl, o=o: e.tensor_tensor(
                        pool[:, 3078:3078 + TA], dsl,
                        bigA[:, o * 512:o * 512 + TA], ALU.mult))
                    W.op("vector", "v", lambda e, dsl=dsl, o=o: e.tensor_tensor(
                        dsl, pool[:, 3078:3078 + TA],
                        bigA[:, o * 512:o * 512 + TA], ALU.add))
                v_ph1 = W.cnt["v"]
                W.wait("scalar", "v", v_ph1)
                for j in range(2):
                    o = grp * 2 + j
                    zsl = stashz[:, j * 512:(j + 1) * 512]
                    dsl = stash[:, (0 + j) * 512:(1 + j) * 512]
                    asl = stash[:, (2 + j) * 512:(3 + j) * 512]
                    bsq = pool[:, 2566:2566 + TA] if j == 0 else pool[:, 3078:3078 + TA]
                    s_b = W.op("scalar", "sc", lambda e, zsl=zsl, bsq=bsq:
                               e.activation(bsq, zsl, AF.Sqrt))
                    W.wait("vector", "sc", s_b)
                    W.opn("vector", lambda e, bsq=bsq, dsl=dsl: e.scalar_tensor_tensor(
                        pool[:, 3590:3590 + TA], bsq, 0.5, dsl, ALU.mult, ALU.mult))
                    rsl = (o % 2) * 512
                    key = (t, o - 2) if o >= 2 else (t - 1, KH - 2 + o)
                    if key in spill_st:
                        W.wait("vector", "st", spill_st[key])
                    W.opn("vector", lambda e, asl=asl, o=o: e.tensor_tensor_scan(
                        pool[:, 4102:4102 + TA], asl, pool[:, 3590:3590 + TA],
                        carry[:, o:o + 1], ALU.mult, ALU.add))
                    W.opn("vector", lambda e, asl=asl, o=o: e.tensor_tensor_scan(
                        pool[:, 4614:4614 + TA], asl, asl,
                        prefp[:, o:o + 1], ALU.mult, ALU.min))
                    W.opn("vector", lambda e, o=o: e.tensor_copy(
                        carry[:, o:o + 1], pool[:, 4102 + TA - 1:4102 + TA]))
                    W.opn("vector", lambda e, o=o: e.tensor_copy(
                        prefp[:, o:o + 1], pool[:, 4614 + TA - 1:4614 + TA]))
                    W.opn("vector", lambda e, rsl=rsl: e.tensor_copy(
                        bigB[:, rsl:rsl + TA], pool[:, 4102:4102 + TA]))
                    v_sp = W.op("vector", "v", lambda e, rsl=rsl: e.tensor_copy(
                        bigB[:, 1024 + rsl:1024 + rsl + TA], pool[:, 4614:4614 + TA]))
                    W.wait("scalar", "v", v_sp)
                    W.dma("scalar", "st",
                          rl_sc[o * 128:(o + 1) * 128, t * TA:(t + 1) * TA],
                          bigB[:, rsl:rsl + TA])
                    spill_st[(t, o)] = W.dma(
                        "scalar", "st",
                        ar_sc[o * 128:(o + 1) * 128, t * TA:(t + 1) * TA],
                        bigB[:, 1024 + rsl:1024 + rsl + TA])
                grp_v[(t, grp)] = W.cnt["v"]
            mm_wawi[t] = W.cnt["mm"]

        W.wait("scalar", "v", W.cnt["v"])
        c_carry = W.dma("scalar", "st", cc_in, carry[:, 0:KH])
        W.wait("gpsimd", "st", c_carry)
        W.cnt["cc"] += 1
        cc1 = W.cnt["cc"]
        W.emit("gpsimd", lambda e: e.collective_compute(
            "AllGather", ALU.bypass, replica_groups=RG8,
            ins=[cc_in], outs=[cc_out]).then_inc(W.handles["cc"], 1))
        W.wait("sync", "cc", cc1)
        W.wait("sync", "mm", W.cnt["mm"])
        W.wait("sync", "v", W.cnt["v"])
        gath = xt[0:8, 0:HID]
        c_gath = W.dma("sync", "x", gath, cc_out)
        W.wait("tensor", "x", c_gath)
        sel_parts = (("A", 0), ("A", 512), ("B", 0), ("B", 512), ("C", 0))
        for hp, (tn, lo) in enumerate(sel_parts):
            pe_guard(tn, lo, lo + 512)
            mm(tn, lo, lo + 512, sel_sb[0:8, 0:1],
               gath[0:8, hp * 512:(hp + 1) * 512], True, True,
               inc=(hp == len(sel_parts) - 1), p0=0, p1=1)
        sel_mm = W.cnt["mm"]
        W.wait("vector", "mm", sel_mm)
        hrow = xt[0:1, 4096:4096 + HID]
        v_h = 0
        for hp, (tn, lo) in enumerate(sel_parts):
            v_h = W.op("vector", "v", lambda e, t=PST[tn], lo=lo, hp=hp:
                       e.tensor_copy(hrow[0:1, hp * 512:(hp + 1) * 512],
                                     t[0:1, lo:lo + 512]))
            mark_read(tn, lo, lo + 512, "v", v_h)
        W.wait("scalar", "v", v_h)
        c_hd = W.dma("scalar", "st", hsel_d, hrow)
        W.wait("sync", "st", c_hd)
        hsel_x = W.dma("sync", "x", hsel[:, :],
                       hsel_d.rearrange("a (p k) -> (a p) k", k=KH))

        v_a2 = [0] * (NTA + 1)
        rlld_v = {}
        rlld_x = {}
        lr_v = {}
        xmid_st = {}

        def a2_ln_parts(t):
            par = (t % 2) * 512
            if t == 0:
                W.wait("sync", "st", c_hd)
                W.wait("sync", "mm", sel_mm)
            W.wait("sync", "v", v_a2[t - 1] if t >= 1 else W.cnt["v"])
            xc = 0
            for k in range(KD):
                xc = W.dma("sync", "x", xt[:, k * 1024 + par:k * 1024 + par + TA],
                           xin[128 * k:128 * (k + 1), t * TA:(t + 1) * TA])
            c_rows = W.dma("sync", "x", rows[0:1, 0:1024], rows_d[t])
            pe_guard("D", 0, 1024)
            W.wait("tensor", "x", c_rows)
            mm("D", 0, TA, ones_k1[0:1, :], rows[0:1, 0:TA], True, True)
            bc = mm("D", 512, 512 + TA, ones_k1[0:1, :], rows[0:1, 512:512 + TA],
                    True, True, inc=True)
            ln_state = {"v": 0, "prev": {}, "bc": bc, "xc": xc}

            def ln_chunk(k):
                xck = xt[:, k * 1024 + par:k * 1024 + par + TA]
                t0 = pv(5120 + (k % 2) * 512, TA)
                if k == 0:
                    W.wait("vector", "mm", ln_state["bc"])
                    W.wait("vector", "x", ln_state["xc"])
                W.opn("vector", lambda e, t0=t0, x=xck: e.tensor_tensor(
                    t0, x, psD[:, 0:TA], ALU.subtract),
                    dep=(None if k < 2 else [ln_state["prev"][k - 2]]))
                v_ln = W.opn("vector", lambda e, t0=t0, k=k: e.tensor_tensor(
                    xnt[:, k * 1024:k * 1024 + TA], t0, psD[:, 512:512 + TA],
                    ALU.mult), dep=W.cnt["v"])
                ln_state["prev"][k] = v_ln
                ln_state["v"] = v_ln
                if k == KD - 1:
                    mark_read("D", 0, 1024, "v", v_ln)
            return ln_state, [lambda k=k: ln_chunk(k) for k in range(KD)]

        ffn_stats_mm = {}

        def ffn_stats_chunk(k, v_r, par):
            xck = xt[:, k * 1024:(k + 1) * 1024]
            xq = xsqb[:, 0:1024]
            W.wait("scalar", "v", v_r)
            if k >= 1:
                W.wait("scalar", "mm", ffn_stats_mm[k - 1])
            csq = W.op("scalar", "sc", lambda e, o=xq, i=xck: e.activation(
                o, i, AF.Square))
            if k == 0:
                pe_guard("A", 0, 1024)
                pe_guard("C", 0, 1024)
            W.wait("tensor", "sc", csq)
            for hh in range(2):
                mm("A", hh * 512, (hh + 1) * 512, ones_col[:, 0:1],
                   xck[:, hh * 512:(hh + 1) * 512], k == 0, k == KD - 1,
                   p0=0, p1=1)
                ffn_stats_mm[k] = mm("C", hh * 512, (hh + 1) * 512,
                                     ones_colb[:, 0:1],
                                     xq[:, hh * 512:(hh + 1) * 512], k == 0,
                                     k == KD - 1, inc=(hh == 1), p0=0, p1=1)

        def ffn_tile(jt):
            W.vbar()
            W.barrier("scalar", ("mm", "v"))
            W.barrier("tensor", ("sc", "v"))
            W.barrier("vector", ("mm", "sc"))
            stats_done = ffn_stats_mm[KD - 1]
            mean_r = rows[0:1, 0:TF]
            rstd_r = rows[0:1, TF:2 * TF]
            W.wait("vector", "mm", stats_done)
            v_mean = W.op("vector", "v", lambda e: e.tensor_scalar(
                mean_r, psA[0:1, 0:TF], 1.0 / C, None, ALU.mult))
            W.wait("scalar", "v", v_mean)
            c_m2 = W.op("scalar", "sc", lambda e: e.activation(
                pool[0:1, 0:TF], mean_r, AF.Square))
            W.wait("vector", "sc", c_m2)
            v_var = W.op("vector", "v", lambda e: e.scalar_tensor_tensor(
                pool[0:1, 0:TF], psC[0:1, 0:TF], 1.0 / C, pool[0:1, 0:TF],
                ALU.mult, ALU.subtract))
            mark_read("A", 0, 1024, "v", v_var)
            mark_read("C", 0, 1024, "v", v_var)
            W.wait("scalar", "v", v_var)
            c_sd = W.op("scalar", "sc", lambda e: e.activation(
                pool[0:1, 0:TF], pool[0:1, 0:TF], AF.Sqrt, bias=1e-5))
            W.wait("vector", "sc", c_sd)
            v_rst = W.op("vector", "v", lambda e: e.reciprocal(
                rstd_r, pool[0:1, 0:TF]))
            pe_guard("C", 0, 1024)
            pe_guard("D", 0, 1024)
            W.wait("tensor", "v", v_rst)
            for hh in range(2):
                mm("C", hh * 512, (hh + 1) * 512, ones_k1[0:1, :],
                   mean_r[0:1, hh * 512:(hh + 1) * 512], True, True)
                bc = mm("D", hh * 512, (hh + 1) * 512, ones_k1[0:1, :],
                        rstd_r[0:1, hh * 512:(hh + 1) * 512], True, True,
                        inc=(hh == 1))
            W.wait("vector", "mm", bc)
            pe_guard("A", 0, 1024)
            rt_mm = {}
            v_xn = 0
            for k in range(KD):
                xnf = pv((k % 2) * 1024, TF)
                xck = xt[:, k * 1024:(k + 1) * 1024]
                if k >= 2:
                    W.wait("vector", "mm", rt_mm[k - 2])
                W.opn("vector", lambda e, xnf=xnf, x=xck: e.tensor_tensor(
                    xnf, x, psC[:, 0:TF], ALU.subtract))
                W.opn("vector", lambda e, xnf=xnf: e.tensor_tensor(
                    xnf, xnf, psD[:, 0:TF], ALU.mult))
                v_xn = W.op("vector", "v", lambda e, xnf=xnf, k=k: e.tensor_copy(
                    xnt[:, k * 1024:(k + 1) * 1024], xnf))
                W.wait("tensor", "v", v_xn)
                for hh in range(2):
                    rt_mm[k] = mm("A", hh * 512, (hh + 1) * 512,
                                  gs_sb[:, k * 9:(k + 1) * 9],
                                  xnf[:, hh * 512:(hh + 1) * 512],
                                  k == 0, k == KD - 1, inc=(hh == 1),
                                  p0=0, p1=9)
            mark_read("C", 0, 1024, "v", v_xn)
            mark_read("D", 0, 1024, "v", v_xn)
            rt_sb = pool[0:9, 4096:4096 + TF]
            W.wait("vector", "mm", rt_mm[KD - 1])
            v_rt = W.op("vector", "v", lambda e: e.tensor_copy(
                rt_sb, psA[0:9, 0:TF]))
            mark_read("A", 0, 1024, "v", v_rt)
            pe_guard("C", 0, 512)
            W.wait("tensor", "v", v_rt)
            tp_c = 0
            for g in range(8):
                W.cnt["mm"] += 1
                tp_c = W.cnt["mm"]
                W.emit("tensor", lambda e, g=g: e.transpose(
                    psC[:, g * 9:(g + 1) * 9], rt_sb[0:9, g * 128:(g + 1) * 128],
                    ident[0:9, 0:9]).then_inc(W.handles["mm"], 1))
            lg = pool[:, 5120:5192]
            cmb = pool[:, 5248:5320]
            W.wait("vector", "mm", tp_c)
            v_lg = W.op("vector", "v", lambda e: e.tensor_copy(lg, psC[:, 0:72]))
            mark_read("C", 0, 512, "v", v_lg)
            sm = pool[:, 5376:5377]
            nm = pool[:, 5377:5378]
            ex8 = pool[:, 5384:5392]
            pr8 = pool[:, 5392:5400]
            eq8 = pool[:, 5400:5408]
            m1c = pool[:, 5408:5409]
            m2c = pool[:, 5409:5410]
            v_cmb = 0
            for g in range(8):
                v_nm = W.op("vector", "v", lambda e, g=g: e.reduce_max(
                    nm, lg[:, g * 9:g * 9 + 8], axis=AX.X, negate=True))
                W.wait("scalar", "v", v_nm)
                s_ex = W.op("scalar", "sc", lambda e, g=g: e.activation(
                    ex8, lg[:, g * 9:g * 9 + 8], AF.Exp, bias=nm[:, 0:1]))
                s_gs = W.op("scalar", "sc", lambda e, g=g: e.activation(
                    nm, lg[:, g * 9 + 8:g * 9 + 9], AF.Tanh, scale=0.5))
                W.wait("vector", "sc", s_gs)
                W.opn("vector", lambda e: e.reduce_sum(sm, ex8, axis=AX.X))
                W.opn("vector", lambda e: e.reciprocal(sm, sm))
                W.opn("vector", lambda e: e.tensor_scalar(
                    pr8, ex8, sm[:, 0:1], None, ALU.mult))
                W.opn("vector", lambda e: e.reduce_max(m1c, pr8, axis=AX.X))
                W.opn("vector", lambda e: e.tensor_scalar(
                    eq8, pr8, m1c[:, 0:1], None, ALU.is_equal))
                W.opn("vector", lambda e: e.scalar_tensor_tensor(
                    eq8, eq8, -2.0, pr8, ALU.mult, ALU.add))
                W.opn("vector", lambda e: e.reduce_max(m2c, eq8, axis=AX.X))
                W.opn("vector", lambda e: e.tensor_scalar(
                    eq8, pr8, m2c[:, 0:1], None, ALU.is_ge))
                W.opn("vector", lambda e, g=g: e.tensor_tensor(
                    cmb[:, g * 9:g * 9 + 8], pr8, eq8, ALU.mult))
                v_cmb = W.op("vector", "v", lambda e, g=g: e.tensor_scalar(
                    cmb[:, g * 9 + 8:g * 9 + 9], nm, 0.5, 0.5, ALU.mult, ALU.add))
            pe_guard("B", 0, 512)
            W.wait("tensor", "v", v_cmb)
            W.cnt["mm"] += 1
            c_tp = W.cnt["mm"]
            W.emit("tensor", lambda e: e.transpose(
                psB[0:72, 0:128], cmb, ident[:, :]).then_inc(W.handles["mm"], 1))
            W.wait("vector", "mm", c_tp)
            cT = pool[0:72, 4096:4096 + 128]
            v_cT = W.op("vector", "v", lambda e: e.tensor_copy(cT, psB[0:72, 0:128]))
            mark_read("B", 0, 512, "v", v_cT)
            W.wait("scalar", "v", v_cT)
            c_cmb = W.dma("scalar", "st", comb_d, cT)

            crow = rows[0:1, 0:TF]
            comb_r = comb_d.rearrange("(g i) p -> i g p", i=9)
            h_v = {}
            acc_v = {}
            prev_cb_mm = 0
            for pr_i in range(6):
                W.wait("vector", "mm", W.cnt["mm"])
                if pr_i == 0:
                    W.vbar()
                for ei in range(2):
                    e_idx = pr_i * 2 + ei
                    row_i = e_idx if e_idx < 8 else 8
                    hbuf = bigA if ei == 0 else bigB
                    W.wait("sync", "st", c_cmb)
                    W.wait("sync", "mm", prev_cb_mm)
                    c_row = W.dma("sync", "x", crow, comb_r[row_i])
                    pe_guard("C", 0, 1024)
                    W.wait("tensor", "x", c_row)
                    mm("C", 0, 512, ones_k1[0:1, :], crow[0:1, 0:512], True, True)
                    cb_mm = mm("C", 512, 1024, ones_k1[0:1, :],
                               crow[0:1, 512:1024], True, True, inc=True)
                    prev_cb_mm = cb_mm
                    W.wait("vector", "mm", cb_mm)
                    cb_v = W.op("vector", "v", lambda e: e.tensor_copy(
                        pv(5120, TF), psC[:, 0:TF]))
                    mark_read("C", 0, 1024, "v", cb_v)
                    for o in range(KF):
                        b1, w1 = load_strip(fc1_t[e_idx, o], C)
                        b2, w2 = load_strip(fc2_t[e_idx, o], C)
                        W.wait("tensor", "w", w2)
                        W.wait("tensor", "v", v_xn)
                        t1 = "A" if o % 2 == 0 else "C"
                        t2 = "B" if o % 2 == 0 else "D"
                        pe_guard(t1, 0, 1024)
                        for k in range(KD):
                            for hh in range(2):
                                a_mm = mm(t1, hh * 512, (hh + 1) * 512,
                                          wv(b1, k),
                                          xnt[:, k * 1024 + hh * 512:
                                              k * 1024 + (hh + 1) * 512],
                                          k == 0, k == KD - 1,
                                          inc=(hh == 1 and k == KD - 1))
                        pe_guard(t2, 0, 1024)
                        for k in range(KD):
                            for hh in range(2):
                                b_mm = mm(t2, hh * 512, (hh + 1) * 512,
                                          wv(b2, k),
                                          xnt[:, k * 1024 + hh * 512:
                                              k * 1024 + (hh + 1) * 512],
                                          k == 0, k == KD - 1,
                                          inc=(hh == 1 and k == KD - 1))
                        strips_done(2)
                        hsl = 2048 + (o % 2) * 1024
                        W.wait("scalar", "mm", a_mm)
                        if (e_idx, o - 2) in h_v:
                            W.wait("scalar", "v", h_v[(e_idx, o - 2)])
                        s_si = W.op("scalar", "sc", lambda e, hsl=hsl, t1=t1:
                                    e.activation(pool[:, hsl:hsl + TF],
                                                 PST[t1][:, 0:TF], AF.Silu))
                        mark_read(t1, 0, 1024, "sc", s_si)
                        W.wait("vector", "sc", s_si)
                        W.wait("vector", "mm", b_mm)
                        dep0 = [h_v.get((e_idx, o - 2)), cb_v]
                        W.opn("vector", lambda e, hsl=hsl, t2=t2: e.tensor_tensor(
                            pool[:, hsl:hsl + TF], pool[:, hsl:hsl + TF],
                            PST[t2][:, 0:TF], ALU.mult), dep=dep0)
                        hv = W.opn("vector", lambda e, hsl=hsl, hb=hbuf, o=o:
                                   e.tensor_tensor(hb[:, o * 1024:(o + 1) * 1024],
                                                   pool[:, hsl:hsl + TF],
                                                   pv(5120, TF), ALU.mult),
                                   dep=W.cnt["v"])
                        mark_read(t2, 0, 1024, "v", hv)
                        h_v[(e_idx, o)] = hv
                v_h_all = W.cnt["v"]
                for o2 in range(KD):
                    bufs = []
                    for ei in range(2):
                        bufs.append(load_strip(prj_t[pr_i * 2 + ei, o2], FM))
                    W.wait("tensor", "w", bufs[1][1])
                    W.wait("tensor", "v", v_h_all)
                    tn = "C" if o2 % 2 == 0 else "D"
                    pe_guard(tn, 0, 1024)
                    p_mm = None
                    for ei in range(2):
                        hb = bigA if ei == 0 else bigB
                        bpr = bufs[ei][0]
                        for k in range(KF):
                            for hh in range(2):
                                p_mm = mm(tn, hh * 512, (hh + 1) * 512,
                                          wv(bpr, k),
                                          hb[:, k * 1024 + hh * 512:
                                             k * 1024 + (hh + 1) * 512],
                                          ei == 0 and k == 0,
                                          ei == 1 and k == KF - 1,
                                          inc=(ei == 1 and k == KF - 1 and hh == 1))
                    strips_done(2)
                    W.wait("vector", "mm", p_mm)
                    v_acc = W.opn("vector", lambda e, tn=tn, o2=o2:
                                  e.tensor_tensor(
                                      xt[:, o2 * 1024:(o2 + 1) * 1024],
                                      xt[:, o2 * 1024:(o2 + 1) * 1024],
                                      PST[tn][:, 0:TF], ALU.add),
                                  dep=[acc_v.get(o2 - 2), acc_v.get(o2)])
                    mark_read(tn, 0, 1024, "v", v_acc)
                    acc_v[o2] = v_acc
            for k in range(KD):
                W.wait("scalar", "v", acc_v[k])
                W.dma("scalar", "st", yout[128 * k:128 * (k + 1), jt * TF:(jt + 1) * TF],
                      xt[:, k * 1024:(k + 1) * 1024])

        if debug:
            W.barrier("scalar", ("mm", "sc", "v"))
            W.dma("scalar", "st", dbg_stash, stash[:, 0:3072])
            W.dma("scalar", "st", dbg_c[:, 0:KH], c_c1[:, :])
            W.dma("scalar", "st", dbg_c[:, KH:2 * KH], c_c2[:, :])
            W.dma("scalar", "st", dbg_pool, pool[:, :])

        W.vbar()
        ln_state, ln_q = a2_ln_parts(0)
        for f in ln_q:
            f()
        ln_q = []

        for t in range(NTA):
            par = (t % 2) * 512
            v_ln = ln_state["v"]

            WLPS = (("A", 0), ("A", 512), ("B", 0), ("B", 512))

            def wl_stageA(o):
                buf, wc = load_strip(wl_t[o], C)
                W.wait("tensor", "w", wc)
                W.wait("tensor", "v", v_ln)
                tn, lo = WLPS[o % 4]
                pe_guard(tn, lo, lo + TA)
                g_mm = None
                for k in range(KD):
                    g_mm = mm(tn, lo, lo + TA, wv(buf, k),
                              xnt[:, k * 1024:k * 1024 + TA],
                              k == 0, k == KD - 1, inc=(k == KD - 1))
                strips_done(1)
                uh = (o % 2) * 512
                W.wait("scalar", "mm", g_mm)
                lr_m2 = lr_v.get((t, o - 2)) if o >= 2 else (
                    lr_v.get((t - 1, KH - 2 + o)) if t > 0 else None)
                if lr_m2:
                    W.wait("scalar", "v", lr_m2)
                W.opn("scalar", lambda e, uh=uh, tn=tn, lo=lo, o=o: e.activation(
                    pool[:, uh:uh + TA], PST[tn][:, lo:lo + TA], AF.Identity,
                    bias=c_bl[:, o:o + 1]))
                s_u = W.op("scalar", "sc", lambda e, uh=uh: e.activation(
                    pool[:, 1024 + uh:1024 + uh + TA], pool[:, uh:uh + TA],
                    AF.Square))
                mark_read(tn, lo, lo + 512, "sc", s_u)
                return s_u, lr_m2

            state = {}
            for o in range(KH + 1):
                if o < KH:
                    rsl = (o % 2) * 512
                    key = (t, o - 2) if o >= 2 else (t - 1, KH - 2 + o)
                    if key in rlld_v:
                        W.wait("sync", "v", rlld_v[key])
                    W.dma("sync", "x", bigA[:, rsl:rsl + TA],
                          rl_sc[o * 128:(o + 1) * 128, t * TA:(t + 1) * TA])
                    rlld_x[(t, o)] = W.dma(
                        "sync", "x", bigA[:, 1024 + rsl:1024 + rsl + TA],
                        ar_sc[o * 128:(o + 1) * 128, t * TA:(t + 1) * TA])
                    s_u, lr_m2 = wl_stageA(o)
                    uh = (o % 2) * 512
                    W.wait("vector", "sc", s_u)
                    v1 = W.opn("vector", lambda e, uh=uh: e.tensor_tensor(
                        pool[:, 2048 + uh:2048 + uh + TA], pool[:, uh:uh + TA],
                        pool[:, 1024 + uh:1024 + uh + TA], ALU.mult),
                        dep=[lr_m2])
                else:
                    v1 = None
                if (o - 1) >= 0:
                    po = o - 1
                    ph = (po % 2) * 512
                    st_b = state[po]
                    W.wait("vector", "sc", st_b["th"])
                    st_b["zz"] = W.opn("vector", lambda e, ph=ph: e.tensor_scalar(
                        pool[:, 4096 + ph:4096 + ph + TA],
                        pool[:, 4096 + ph:4096 + ph + TA],
                        0.5, 0.5, ALU.mult, ALU.add),
                        dep=[lr_v.get((t, po - 2)) if po >= 2 else
                             (lr_v.get((t - 1, KH - 2 + po)) if t > 0 else None),
                             st_b["vw"]])
                if o < KH:
                    vw = W.opn("vector", lambda e, uh=uh: e.scalar_tensor_tensor(
                        pool[:, 3072 + uh:3072 + uh + TA],
                        pool[:, 2048 + uh:2048 + uh + TA], 0.044715,
                        pool[:, uh:uh + TA], ALU.mult, ALU.add), dep=[v1])
                    W.wait("scalar", "v", vw)
                    s_th = W.op("scalar", "sc", lambda e, uh=uh: e.activation(
                        pool[:, 4096 + uh:4096 + uh + TA],
                        pool[:, 3072 + uh:3072 + uh + TA], AF.Tanh, scale=GC))
                    state[o] = {"vw": vw, "th": s_th, "v1": v1}
                if (o - 1) >= 0:
                    po = o - 1
                    ph = (po % 2) * 512
                    st_b = state[po]
                    W.opn("vector", lambda e, ph=ph: e.tensor_tensor(
                        pool[:, ph:ph + TA], pool[:, ph:ph + TA],
                        pool[:, 4096 + ph:4096 + ph + TA], ALU.mult),
                        dep=[st_b["zz"], st_b["vw"]])
                    W.wait("vector", "x", rlld_x[(t, po)])
                    if t == 0 and po == 0:
                        W.wait("vector", "x", hsel_x)
                    if t == 0 and po < 4:
                        W.wait("vector", "st", c_carry)
                    prl = (po % 2) * 512
                    W.opn("vector", lambda e, prl=prl, po=po, ph=ph:
                          e.scalar_tensor_tensor(
                              pool[:, 1024 + ph:1024 + ph + TA],
                              bigA[:, 1024 + prl:1024 + prl + TA],
                              hsel[:, po:po + 1], bigA[:, prl:prl + TA],
                              ALU.mult, ALU.add), dep=[st_b["v1"]])
                    lr = W.opn("vector", lambda e, po=po, ph=ph: e.tensor_tensor(
                        bigB[:, po * 512:po * 512 + TA], pool[:, ph:ph + TA],
                        pool[:, 1024 + ph:1024 + ph + TA], ALU.mult),
                        dep=W.cnt["v"])
                    rlld_v[(t, po)] = lr
                    lr_v[(t, po)] = lr
            v_lr_all = W.cnt["v"]

            if t % 2 == 0:
                ln_state, ln_q = a2_ln_parts(t + 1)
            else:
                ln_q = []

            for o2 in range(KD):
                buf, wc = load_strip(wo_t[o2], HID)
                W.wait("tensor", "w", wc)
                W.wait("tensor", "v", v_lr_all)
                lo = (o2 % 2) * 512
                pe_guard("B", lo, lo + TA)
                for k in range(KH):
                    g_mm = mm("B", lo, lo + TA, wv(buf, k),
                              bigB[:, k * 512:k * 512 + TA],
                              k == 0, k == KH - 1, inc=(k == KH - 1))
                strips_done(1)
                W.wait("vector", "mm", g_mm)
                v_r = W.opn("vector", lambda e, lo=lo, o2=o2, par=par:
                            e.scalar_tensor_tensor(
                                xt[:, o2 * 1024 + par:o2 * 1024 + par + TA],
                                psB[:, lo:lo + TA],
                                c_bo[:, o2:o2 + 1],
                                xt[:, o2 * 1024 + par:o2 * 1024 + par + TA],
                                ALU.add, ALU.add))
                mark_read("B", lo, lo + 512, "v", v_r)
                if t % 2 == 1:
                    ffn_stats_chunk(o2, v_r, par)
                else:
                    for _ in range(2):
                        if ln_q:
                            ln_q.pop(0)()
            v_a2[t] = W.cnt["v"]
            if t == 1:
                ffn_tile(0)
                v_a2[1] = W.cnt["v"]
                W.vbar()
                W.wait("vector", "mm", W.cnt["mm"])
                W.wait("sync", "st", W.cnt["st"])
                ln_state, ln_q = a2_ln_parts(2)
                for f in ln_q:
                    f()
                ln_q = []
            if t == 3:
                ffn_tile(1)
        W.wait("scalar", "st", W.cnt["st"])
        W.wait("sync", "st", W.cnt["st"])

        with ExitStack() as es2:
            for s in W.cnt:
                W.handles[s] = es2.enter_context(nc.semaphore(s))
            block = es2.enter_context(nc.Block())
            W.run(block)

    return nc


def _prep_inputs(kw):
    f32 = np.float32
    x = kw["L0_attn_x"]
    xT = np.ascontiguousarray(np.asarray(x, dtype=f32).transpose(0, 2, 1))

    def tile_w(w, kchunks, ochunks):
        r = np.asarray(w, dtype=f32).reshape(kchunks, 128, ochunks, 128)
        return np.ascontiguousarray(
            r.transpose(2, 1, 0, 3).reshape(ochunks, 128, kchunks * 128))

    wr = tile_w(kw["L0_attn_Wr"], KD, KH)
    wl = tile_w(kw["L0_attn_Wl"], KD, KH)
    wa = tile_w(kw["L0_attn_Wa"], KH, KH)
    wi = tile_w(kw["L0_attn_Wi"], KH, KH)
    wo = tile_w(kw["L0_attn_Wo"], KH, KD)

    fc1 = np.concatenate([np.asarray(kw["L0_ffn_expert_fc1"], f32),
                          np.asarray(kw["L0_ffn_shared_fc1"], f32).reshape(4, FM, C)],
                         axis=0)
    fc2 = np.concatenate([np.asarray(kw["L0_ffn_expert_fc2"], f32),
                          np.asarray(kw["L0_ffn_shared_fc2"], f32).reshape(4, FM, C)],
                         axis=0)
    prj = np.concatenate([np.asarray(kw["L0_ffn_expert_proj"], f32),
                          np.asarray(kw["L0_ffn_shared_proj"], f32)
                          .reshape(C, 4, FM).transpose(1, 0, 2)], axis=0)
    fc1_t = np.stack([tile_w(fc1[e].T, KD, KF) for e in range(NE)])
    fc2_t = np.stack([tile_w(fc2[e].T, KD, KF) for e in range(NE)])
    prj_t = np.stack([tile_w(prj[e].T, KF, KD) for e in range(NE)])

    gct = np.concatenate([np.asarray(kw["L0_ffn_gate_w"], f32),
                          np.asarray(kw["L0_ffn_shared_gate_w"], f32)], axis=0)
    gs = np.ascontiguousarray(
        gct.T.reshape(KD, 128, 9).transpose(1, 0, 2).reshape(128, KD * 9))

    def vecT(v, kchunks):
        return np.ascontiguousarray(np.asarray(v, f32).reshape(kchunks, 128).T)

    consts = dict(
        blT=vecT(kw["L0_attn_bl"], KH), brT=vecT(kw["L0_attn_br"], KH),
        cbT=vecT(kw["L0_attn_conv_b"], KH), baT=vecT(kw["L0_attn_ba"], KH),
        biT=vecT(kw["L0_attn_bi"], KH), lamT=vecT(kw["L0_attn_lam"], KH),
        boT=vecT(kw["L0_attn_bo"], KD),
        cwT=np.ascontiguousarray(np.concatenate(
            [vecT(np.asarray(kw["L0_attn_conv_w"], f32)[t, 0], KH)
             for t in range(4)], axis=1)),
        identT=np.eye(128, dtype=f32),
        gs_t=gs,
        fconst=np.tile(np.array([[-1.0, 1.0, 1e-5, 0.5, GC, 0.0, 2.0, -2.0]], f32), (128, 1)),
    )

    in_maps = []
    for c in range(8):
        b, h = c // 2, c % 2
        m = dict(xin=np.ascontiguousarray(xT[b, :, h * TC:(h + 1) * TC]),
                 wr_t=wr, wl_t=wl, wa_t=wa, wi_t=wi, wo_t=wo,
                 fc1_t=fc1_t, fc2_t=fc2_t, prj_t=prj_t)
        if h == 1:
            xpre = xT[b, :, h * TC - 3:h * TC]
        else:
            xpre = np.zeros((C, 3), f32)
        m["xpreT"] = np.ascontiguousarray(xpre.reshape(KD, 128, 3))
        m["maskT"] = np.full((128, 1), float(h), f32)
        sel = np.zeros((8, 1), f32)
        if h == 1:
            sel[c - 1, 0] = 1.0
        m["selT"] = sel
        m.update(consts)
        in_maps.append(m)
    return in_maps


def kernel(**kw):
    in_maps = _prep_inputs(kw)
    nc = build_program()
    res = run_bass_kernel_spmd(nc, in_maps, core_ids=list(range(8)))
    out = np.empty((4, 4096, 2048), np.float32)
    for c in range(8):
        b, h = c // 2, c % 2
        out[b, h * TC:(h + 1) * TC, :] = res.results[c]["yout"].T
    return out
